# version 01
# baseline (speedup 1.0000x reference)
import os
import numpy as np
import ml_dtypes
from contextlib import ExitStack

import concourse.bass as bass
import concourse.mybir as mybir
from concourse.bass_utils import run_bass_kernel_spmd

F32 = mybir.dt.float32
BF16 = mybir.dt.bfloat16
I32 = mybir.dt.int32
AF = mybir.ActivationFunctionType
ALU = mybir.AluOpType

B, S, D = 4, 2048, 2048
NHQ, NKV, HD = 32, 8, 64
HID = 5632
THETA = 10000.0
EPS = 1e-5
ND = D // 128
NH = HID // 128
NO = 1024
G = 4
NG = NH // G
EXPB = -24.0

_EVEN = [h for h in range(NHQ) if (h // 4) % 2 == 0]
_ODD = [h for h in range(NHQ) if (h // 4) % 2 == 1]
SLOT_HEADS = []
for _i in range(NHQ // 2):
    SLOT_HEADS.append(_EVEN[_i])
    SLOT_HEADS.append(_ODD[_i])
ROPE_PERM = [2 * j for j in range(32)] + [2 * j + 1 for j in range(32)]

CHUNKS0 = [0, 1, 2, 3, 8, 9, 10, 11]
CHUNKS1 = list(range(16))
MASKABLE0 = CHUNKS0
MASKABLE1 = [4, 5, 6, 7, 12, 13, 14, 15]


def _own_blocks(half):
    return [(0, 512), (1536, 2048)] if half == 0 else [(512, 1024), (1024, 1536)]


class Prog:
    DMA_FAMS = ("AW", "FG", "FU", "FD", "MG", "MS", "OS")

    def __init__(self, nc, st):
        self.nc = nc
        self.ops = {k: [] for k in ("pe", "act", "dve", "qg", "qs")}
        self.sems = {}
        self.cnt = {}
        self.waited = {}
        for s in ("PE", "ACT", "DVE"):
            self.sems[s] = st.enter_context(nc.semaphore(s))
            self.cnt[s] = 0
        self.fam_idx = {}
        for f in self.DMA_FAMS:
            for par in (0, 1):
                name = f + str(par)
                self.sems[name] = st.enter_context(nc.semaphore(name))
                self.cnt[name] = 0
            self.fam_idx[f] = 0

    def dma(self, eng, fam, fn, wait=()):
        idx = self.fam_idx[fam]
        self.fam_idx[fam] = idx + 1
        sname = fam + str(idx % 2)
        extra = []
        if idx >= 2:
            extra = [(sname, 16 * (idx // 2))]
        self.cnt[sname] += 16
        ev = (sname, self.cnt[sname])
        return self.op(eng, fn, wait=list(wait) + extra, sem=sname, n=16,
                       _preinc=True) or ev

    def op(self, eng, fn, wait=(), sem=None, n=1, _preinc=False):
        wait = list(wait)
        if eng == "dve":
            if sem is None:
                sem = "DVE"
            wait.append(("DVE", self.cnt["DVE"]))
        elif eng == "act":
            if sem is None:
                sem = "ACT"
            wait.append(("ACT", self.cnt["ACT"]))
        wa = []
        for evt in wait:
            if evt is None:
                continue
            s, c = evt
            k = (eng, s)
            if self.waited.get(k, 0) < c:
                self.waited[k] = c
                wa.append((self.sems[s], c))
        ev = None
        mark = None
        if sem is not None:
            if not _preinc:
                self.cnt[sem] += n
            ev = (sem, self.cnt[sem])
            mark = (self.sems[sem], n)

        def run(e, wa=wa, fn=fn, mark=mark):
            for h, c in wa:
                e.wait_ge(h, c)
            r = fn(e)
            if mark is not None:
                r.then_inc(*mark)
        self.ops[eng].append(run)
        return ev

    def wait_only(self, eng, wait):
        wa = []
        for evt in wait:
            if evt is None:
                continue
            s, c = evt
            k = (eng, s)
            if self.waited.get(k, 0) < c:
                self.waited[k] = c
                wa.append((self.sems[s], c))
        if wa:
            def run(e, wa=wa):
                for h, c in wa:
                    e.wait_ge(h, c)
            self.ops[eng].append(run)

    def barrier(self):
        evs = [(s, self.cnt[s]) for s in ("PE", "ACT", "DVE") if self.cnt[s] > 0]
        for eng in ("pe", "act", "dve"):
            self.wait_only(eng, evs)


def build_program():
    nc = bass.Bass("TRN2", target_bir_lowering=False, debug=False)
    x_d = nc.dram_tensor("x", [128, ND * S], F32, kind="ExternalInput").ap()
    wq_d = nc.dram_tensor("wq", [16, 128, 2048], F32, kind="ExternalInput").ap()
    wk_d = nc.dram_tensor("wk", [4, 128, 2048], F32, kind="ExternalInput").ap()
    wv_d = nc.dram_tensor("wv", [16, 128, 512], F32, kind="ExternalInput").ap()
    wo_d = nc.dram_tensor("wo", [16, 128, 2048], F32, kind="ExternalInput").ap()
    wg_d = nc.dram_tensor("wg", [NH, 128, 2048], I32, kind="ExternalInput").ap()
    wu_d = nc.dram_tensor("wu", [NH, 128, 2048], I32, kind="ExternalInput").ap()
    wd_d = nc.dram_tensor("wd", [NH, 128, 2048], I32, kind="ExternalInput").ap()
    sg_d = nc.dram_tensor("sg", [128, NH], F32, kind="ExternalInput").ap()
    su_d = nc.dram_tensor("su", [128, NH], F32, kind="ExternalInput").ap()
    sd_d = nc.dram_tensor("sd", [128, ND], F32, kind="ExternalInput").ap()
    onesb_d = nc.dram_tensor("onesb", [128, 1], F32, kind="ExternalInput").ap()
    negb_d = nc.dram_tensor("negb", [1, 128], F32, kind="ExternalInput").ap()
    onesr_d = nc.dram_tensor("onesr", [1, 128], F32, kind="ExternalInput").ap()
    ones64_d = nc.dram_tensor("ones64", [1, 64], F32, kind="ExternalInput").ap()
    psw_d = nc.dram_tensor("psw", [128, 128], F32, kind="ExternalInput").ap()
    ck_d = nc.dram_tensor("ck", [128, S], F32, kind="ExternalInput").ap()
    sk_d = nc.dram_tensor("sk", [128, S], F32, kind="ExternalInput").ap()
    msk_d = nc.dram_tensor("msk", [128, 16 * 512], F32, kind="ExternalInput").ap()
    out_d = nc.dram_tensor("out", [128, ND * NO], F32, kind="ExternalOutput").ap()

    st = ExitStack()
    with st:
        def sb(name, shape, dt):
            return st.enter_context(nc.sbuf_tensor(name, shape, dt))
        p = Prog(nc, st)

        OnesB = sb("OnesB", [128, 1], BF16)
        NegB = sb("NegB", [1, 128], BF16)
        OnesR = sb("OnesR", [1, 128], BF16)
        Ones64 = sb("Ones64", [1, 64], BF16)
        Sg = sb("Sg", [128, NH], F32)
        Su = sb("Su", [128, NH], F32)
        Sd = sb("Sd", [128, ND], F32)
        Xb = sb("Xb", [128, ND * NO], BF16)
        WA = [sb(f"WA{i}", [128, 2048], BF16) for i in range(2)]
        SC32 = [sb(f"SC32{i}", [128, 512], F32) for i in range(2)]
        SCB = [sb(f"SCB{i}", [128, 512], BF16) for i in range(2)]
        SQB = [sb(f"SQB{i}", [128, 512], BF16) for i in range(2)]

        st_qkv = ExitStack()
        qT = st_qkv.enter_context(nc.sbuf_tensor("qT", [128, 16 * NO], BF16))
        kT = st_qkv.enter_context(nc.sbuf_tensor("kT", [128, 4 * S], BF16))
        Vst = st_qkv.enter_context(nc.sbuf_tensor("Vst", [128, 16 * 520], BF16))

        st_a1 = ExitStack()
        Xf = st_a1.enter_context(nc.sbuf_tensor("Xf", [128, ND * S], BF16))
        Psw = st_a1.enter_context(nc.sbuf_tensor("Psw", [128, 128], BF16))
        Ck = st_a1.enter_context(nc.sbuf_tensor("Ck", [128, S], F32))
        Sk = st_a1.enter_context(nc.sbuf_tensor("Sk", [128, S], F32))

        p.dma("qg", "MG", lambda e: e.dma_start(OnesB[:], onesb_d))
        p.dma("qg", "MG", lambda e: e.dma_start(NegB[:], negb_d))
        p.dma("qg", "MG", lambda e: e.dma_start(OnesR[:], onesr_d))
        ev_cst = p.dma("qg", "MG", lambda e: e.dma_start(Ones64[:], ones64_d))
        ev_psw = p.dma("qg", "MG", lambda e: e.dma_start(Psw[:], psw_d))
        ev_xf = p.dma("qg", "MG", lambda e: e.dma_start(Xf[:], x_d))

        ev_sg = p.dma("qs", "MS", lambda e: e.dma_start(Sg[:], sg_d))
        ev_su = p.dma("qs", "MS", lambda e: e.dma_start(Su[:], su_d))
        ev_sd = p.dma("qs", "MS", lambda e: e.dma_start(Sd[:], sd_d))
        ev_ck = p.dma("qs", "MS", lambda e: e.dma_start(Ck[:], ck_d))
        ev_sk = p.dma("qs", "MS", lambda e: e.dma_start(Sk[:], sk_d))
        ev_vone = p.op("dve", lambda e: e.memset(
            Vst[:].rearrange("p (a g c) -> p a g c", g=8, c=65)[:, :, :, 64:65], 1.0),
            sem="DVE")

        items = [("q", i) for i in range(16)] + [("k", i) for i in range(4)] + \
                [("v", i % 16) for i in range(32)]
        ev_w = [None] * len(items)
        ev_cons = [None] * len(items)

        def dma_item(i):
            war = None
            if i >= 2:
                war = ev_cons[i - 2]
                assert war is not None, f"item {i} WAR missing"
            kind, idx = items[i]
            if kind == "q":
                src, dst = wq_d[idx], WA[i % 2][:]
            elif kind == "k":
                src, dst = wk_d[idx], WA[i % 2][:]
            else:
                src, dst = wv_d[idx], WA[i % 2][:, 0:512]
            ev_w[i] = p.dma("qg", "AW", (lambda src, dst: lambda e: e.dma_start(dst, src))(src, dst),
                            wait=[war])

        dma_item(0)
        dma_item(1)

        with ExitStack() as ps_a:
            Pq = [ps_a.enter_context(nc.psum_tensor(f"Pq{i}", [128, 512], F32)) for i in range(2)]
            Pw = [ps_a.enter_context(nc.psum_tensor(f"Pw{i}", [128, 512], F32)) for i in range(2)]

            steps = []
            for i, (kind, idx) in enumerate(items):
                if kind == "v":
                    continue
                nsub = 2 if kind == "q" else 4
                for sub in range(nsub):
                    steps.append((kind, i, idx, sub))

            ev_mm = [None] * len(steps)
            ev_cp = [None] * len(steps)
            ev_rp = [None] * len(steps)
            for r, (kind, i, idx, sub) in enumerate(steps):
                ri = r % 2
                first_wait = [ev_w[i], ev_xf, ev_rp[r - 2] if r >= 2 else None]
                for dc in range(16):
                    off = sub * 512 if kind == "q" else sub * 512
                    rhs = Xf[:, dc * S + off: dc * S + off + 512]
                    mm = p.op("pe", (lambda ri, dc, rhs, i: lambda e: e.matmul(
                        Pq[ri][:], WA[i % 2][:, dc * 128:(dc + 1) * 128], rhs,
                        start=(dc == 0), stop=(dc == 15)))(ri, dc, rhs, i),
                        wait=first_wait if dc == 0 else [],
                        sem="PE" if dc == 15 else None)
                ev_mm[r] = mm
                if sub == (1 if kind == "q" else 3):
                    ev_cons[i] = mm
                    if i + 2 < len(items) and items[i + 2][0] != "v":
                        dma_item(i + 2)
                ev_cp[r] = p.op("act", (lambda ri: lambda e: e.copy(SCB[ri][:], Pq[ri][:]))(ri),
                                wait=[ev_mm[r], ev_rp[r - 2] if r >= 2 else None], sem="ACT")
                ev_sw = p.op("pe", (lambda ri: lambda e: e.matmul(
                    Pw[ri][:], Psw[:], SCB[ri][:], start=True, stop=True))(ri),
                    wait=[ev_cp[r], ev_psw, ev_rp[r - 2] if r >= 2 else None], sem="PE")
                cab = Ck[:, sub * 512:(sub + 1) * 512]
                sab = Sk[:, sub * 512:(sub + 1) * 512]
                if kind == "q":
                    dst = qT[:, idx * NO + sub * 512: idx * NO + (sub + 1) * 512]
                else:
                    dst = kT[:, idx * S + sub * 512: idx * S + (sub + 1) * 512]
                p.op("dve", (lambda ri, cab: lambda e: e.tensor_tensor(
                    SC32[ri][:], Pq[ri][:], cab, ALU.mult))(ri, cab),
                    wait=[ev_mm[r], ev_cp[r], ev_ck, ev_sk], sem="DVE")
                p.op("dve", (lambda ri, sab: lambda e: e.tensor_tensor(
                    SCB[ri][:], Pw[ri][:], sab, ALU.mult))(ri, sab),
                    wait=[ev_sw], sem="DVE")
                ev_rp[r] = p.op("dve", (lambda ri, dst: lambda e: e.tensor_tensor(
                    dst, SC32[ri][:], SCB[ri][:], ALU.add))(ri, dst), sem="DVE")
            ev_rope_done = ev_rp[-1]

        dma_item(20)
        dma_item(21)
        with ExitStack() as ps_v:
            Pv = [ps_v.enter_context(nc.psum_tensor(f"Pv{i}", [128, 512], F32)) for i in range(8)]
            ev_vcp = {}
            vit = 20
            for pa in range(2):
                for dc in range(16):
                    i = vit + pa * 16 + dc
                    vw = [ev_w[i]]
                    if pa == 0 and dc == 0:
                        vw += [ev_xf, ev_rope_done]
                    if pa == 1 and dc == 0:
                        vw += [ev_vcp[kk2] for kk2 in range(8)]
                    last = None
                    for kk in range(8):
                        kc = pa * 8 + kk
                        last = p.op("pe", (lambda kk, dc, kc, i: lambda e: e.matmul(
                            Pv[kk][:], Xf[:, dc * S + kc * 128: dc * S + (kc + 1) * 128],
                            WA[i % 2][:, 0:512],
                            start=(dc == 0), stop=(dc == 15)))(kk, dc, kc, i),
                            wait=vw if kk == 0 else [],
                            sem="PE" if kk == 7 else None)
                    ev_cons[i] = last
                    if i + 2 < len(items):
                        dma_item(i + 2)
                for kk in range(8):
                    kc = pa * 8 + kk
                    dstv = Vst[:].rearrange("p (a g c) -> p a g c", g=8, c=65)[:, kc:kc + 1, :, 0:64]
                    srcv = Pv[kk][:].rearrange("p (a g c) -> p a g c", a=1, g=8, c=64)
                    ev_vcp[kc] = p.op("act", (lambda dstv, srcv: lambda e: e.copy(dstv, srcv))(dstv, srcv),
                                      wait=[ev_cons[vit + pa * 16 + 15], ev_vone], sem="ACT")

        st_a1.close()
        st_a2 = ExitStack()
        Msk = st_a2.enter_context(nc.sbuf_tensor("Msk", [128, 16 * 512], BF16))
        ExpT = st_a2.enter_context(nc.sbuf_tensor("ExpT", [128, 3 * 512], BF16))
        RB = [st_a2.enter_context(nc.sbuf_tensor(f"RB{i}", [128, 512], F32)) for i in range(2)]
        Row = st_a2.enter_context(nc.sbuf_tensor("Row", [1, 4 * 512], F32))
        RowB = st_a2.enter_context(nc.sbuf_tensor("RowB", [1, 4 * 512], BF16))
        st_ao = ExitStack()
        AoT = st_ao.enter_context(nc.sbuf_tensor("AoT", [128, 16 * NO], BF16, side="right"))
        ev_a1_pe = ("PE", p.cnt["PE"])
        ev_msk = p.dma("qg", "MG", lambda e: e.dma_start(Msk[:], msk_d), wait=[ev_a1_pe])

        p.barrier()

        with ExitStack() as ps_a2:
            Ps = [ps_a2.enter_context(nc.psum_tensor(f"Ps{i}", [128, 512], F32)) for i in range(2)]
            Po = [ps_a2.enter_context(nc.psum_tensor(f"Po{i}", [128, 512], F32)) for i in range(2)]
            Pden = [ps_a2.enter_context(nc.psum_tensor(f"Pden{i}", [1, 512], F32)) for i in range(2)]
            Pr = [ps_a2.enter_context(nc.psum_tensor(f"Pr{i}", [128, 512], F32)) for i in range(2)]

            ps_last_exp = {}
            exp_last_pv = {}
            ev_norm = [None] * 64
            ev_rbcp = [None] * 64
            for s_ in range(32):
                g = SLOT_HEADS[s_] // 4
                par = s_ % 2
                base = 64 * par
                for b_ in range(2):
                    it = s_ * 2 + b_
                    pi = it % 2
                    chunks = CHUNKS0 if b_ == 0 else CHUNKS1
                    mables = MASKABLE0 if b_ == 0 else MASKABLE1
                    nch = len(chunks)
                    ev_pvdone = None
                    for ci, kc in enumerate(chunks):
                        si = ci % 2
                        ei = ci % 3
                        lk = kT[base:base + 64, (g // 2) * S + kc * 128:(g // 2) * S + (kc + 1) * 128]
                        rq = qT[base:base + 64, (s_ // 2) * NO + b_ * 512:(s_ // 2) * NO + (b_ + 1) * 512]
                        ev_smm = p.op("pe", (lambda lk, rq, si: lambda e: e.matmul(
                            Ps[si][:], lk, rq, start=True, stop=True))(lk, rq, si),
                            wait=[ps_last_exp.get(si)], sem="PE")
                        ev_e = p.op("act", (lambda si, ei: lambda e: e.activation(
                            ExpT[:, ei * 512:(ei + 1) * 512], Ps[si][:], AF.Exp,
                            bias=np.float32(EXPB), scale=np.float32(0.125)))(si, ei),
                            wait=[ev_smm, exp_last_pv.get(ei)], sem="ACT")
                        ps_last_exp[si] = ev_e
                        if kc in mables:
                            mslot = MASKABLE0.index(kc) if b_ == 0 else 8 + MASKABLE1.index(kc)
                            ev_e = p.op("dve", (lambda ei, mslot: lambda e: e.tensor_tensor(
                                ExpT[:, ei * 512:(ei + 1) * 512], ExpT[:, ei * 512:(ei + 1) * 512],
                                Msk[:, mslot * 512:(mslot + 1) * 512], ALU.mult))(ei, mslot),
                                wait=[ev_e, ev_msk], sem="DVE")
                        pvwait = [ev_e]
                        if ci == 0:
                            pvwait.append(ev_norm[it - 2] if it >= 2 else None)
                        if par == 0:
                            lv = Vst[:, kc * 520 + g * 65: kc * 520 + (g + 1) * 65]
                            mmo = p.op("pe", (lambda lv, ei, pi, ci, nch: lambda e: e.matmul(
                                Po[pi][0:65, :], lv, ExpT[:, ei * 512:(ei + 1) * 512],
                                start=(ci == 0), stop=(ci == nch - 1)))(lv, ei, pi, ci, nch),
                                wait=pvwait, sem="PE")
                        else:
                            lv = Vst[:, kc * 520 + g * 65: kc * 520 + g * 65 + 64]
                            p.op("pe", (lambda lv, ei, pi, ci, nch: lambda e: e.matmul(
                                Po[pi][64:128, :], lv, ExpT[:, ei * 512:(ei + 1) * 512],
                                start=(ci == 0), stop=(ci == nch - 1)))(lv, ei, pi, ci, nch),
                                wait=pvwait)
                            mmo = p.op("pe", (lambda ei, pi, ci, nch: lambda e: e.matmul(
                                Pden[pi][:], OnesB[:], ExpT[:, ei * 512:(ei + 1) * 512],
                                start=(ci == 0), stop=(ci == nch - 1)))(ei, pi, ci, nch),
                                sem="PE")
                        exp_last_pv[ei] = mmo
                        ev_pvdone = mmo
                    den_src = Po[pi][64:65, :] if par == 0 else Pden[pi][0:1, :]
                    ev_dcp = p.op("act", (lambda den_src, pi: lambda e: e.copy(
                        Row[:, pi * 512:(pi + 1) * 512], den_src))(den_src, pi),
                        wait=[ev_pvdone, ev_norm[it - 2] if it >= 2 else None], sem="ACT")
                    ev_rc = p.op("dve", (lambda pi: lambda e: e.reciprocal(
                        Row[:, (2 + pi) * 512:(3 + pi) * 512], Row[:, pi * 512:(pi + 1) * 512]))(pi),
                        wait=[ev_dcp], sem="DVE")
                    ev_rh = p.op("act", (lambda pi: lambda e: e.copy(
                        RowB[:, pi * 512:(pi + 1) * 512], Row[:, (2 + pi) * 512:(3 + pi) * 512]))(pi),
                        wait=[ev_rc], sem="ACT")
                    ev_rl = p.op("dve", (lambda pi: lambda e: e.tensor_tensor(
                        RowB[:, (2 + pi) * 512:(3 + pi) * 512], Row[:, (2 + pi) * 512:(3 + pi) * 512],
                        RowB[:, pi * 512:(pi + 1) * 512], ALU.subtract))(pi),
                        wait=[ev_rh], sem="DVE")
                    p.op("pe", (lambda pi, base: lambda e: e.matmul(
                        Pr[pi][base:base + 64, :], Ones64[:], RowB[:, pi * 512:(pi + 1) * 512],
                        start=True, stop=False))(pi, base),
                        wait=[ev_rh, ev_cst, ev_rbcp[it - 2] if it >= 2 else None])
                    ev_bc = p.op("pe", (lambda pi, base: lambda e: e.matmul(
                        Pr[pi][base:base + 64, :], Ones64[:], RowB[:, (2 + pi) * 512:(3 + pi) * 512],
                        start=False, stop=True))(pi, base),
                        wait=[ev_rl], sem="PE")
                    ev_rbcp[it] = p.op("act", (lambda pi, base: lambda e: e.copy(
                        RB[pi][base:base + 64, :], Pr[pi][base:base + 64, :]))(pi, base),
                        wait=[ev_bc], sem="ACT")
                    dsta = AoT[base:base + 64, (s_ // 2) * NO + b_ * 512:(s_ // 2) * NO + (b_ + 1) * 512]
                    srcp = Po[pi][0:64, :] if par == 0 else Po[pi][64:128, :]
                    ev_norm[it] = p.op("dve", (lambda dsta, srcp, pi, base: lambda e: e.tensor_tensor(
                        dsta, srcp, RB[pi][base:base + 64, :], ALU.mult))(dsta, srcp, pi, base),
                        wait=[ev_rbcp[it]], sem="DVE")

        ev_a2_pe = ("PE", p.cnt["PE"])
        st_a2.close()
        st_qkv.close()
        X32 = sb("X32", [128, ND * NO], F32)
        RowL = sb("RowL", [1, 4 * 512], F32)
        RowBL = sb("RowBL", [1, 4 * 512], BF16)
        ev_x32 = p.dma("qs", "MS", lambda e: e.dma_start(
            X32[:].rearrange("p (a t) -> p a t", t=NO),
            x_d[:].rearrange("p (a t) -> p a t", t=S)[:, :, 0:NO]),
            wait=[ev_a2_pe, ev_norm[63]])

        p.barrier()

        def emit_ln(tag, get_stats_ev, make_xb, Pst, Row, RowB):
            evn = {}
            with ExitStack() as pst2:
                Pm = pst2.enter_context(nc.psum_tensor("Pm" + tag, [128, 512], F32))
                Pr2 = pst2.enter_context(nc.psum_tensor("Pr2" + tag, [128, 512], F32))
                last_dve_tail = None
                last_act_tail = None
                for th in range(2):
                    ev1 = p.op("dve", (lambda th: lambda e: e.tensor_scalar(
                        Row[:, 0:512], Pst[:, th * 512:(th + 1) * 512],
                        1.0 / D, 0.0, ALU.mult, ALU.add))(th),
                        wait=[get_stats_ev, last_dve_tail, last_act_tail], sem="DVE")
                    p.op("dve", (lambda th: lambda e: e.tensor_scalar(
                        Row[:, 512:1024], Pst[:, (2 + th) * 512:(3 + th) * 512],
                        1.0 / D, 0.0, ALU.mult, ALU.add))(th), sem="DVE")
                    p.op("dve", lambda e: e.tensor_tensor(
                        Row[:, 1024:1536], Row[:, 0:512], Row[:, 0:512], ALU.mult), sem="DVE")
                    ev4 = p.op("dve", lambda e: e.tensor_tensor(
                        Row[:, 512:1024], Row[:, 512:1024], Row[:, 1024:1536], ALU.subtract), sem="DVE")
                    p.op("act", lambda e: e.activation(
                        Row[:, 1024:1536], Row[:, 512:1024], AF.Ln,
                        bias=np.float32(EPS), scale=np.float32(1.0)), wait=[ev4], sem="ACT")
                    ev6 = p.op("act", lambda e: e.activation(
                        Row[:, 1536:2048], Row[:, 1024:1536], AF.Exp,
                        bias=np.float32(0.0), scale=np.float32(-0.5)), sem="ACT")
                    eh1 = p.op("act", lambda e: e.copy(RowB[:, 0:512], Row[:, 0:512]),
                               wait=[ev1], sem="ACT")
                    el1 = p.op("dve", lambda e: e.tensor_tensor(
                        RowB[:, 512:1024], Row[:, 0:512], RowB[:, 0:512], ALU.subtract),
                        wait=[eh1], sem="DVE")
                    eh2 = p.op("act", lambda e: e.copy(RowB[:, 1024:1536], Row[:, 1536:2048]),
                               wait=[ev6], sem="ACT")
                    el2 = p.op("dve", lambda e: e.tensor_tensor(
                        RowB[:, 1536:2048], Row[:, 1536:2048], RowB[:, 1024:1536], ALU.subtract),
                        wait=[eh2], sem="DVE")
                    last_act_tail = eh2
                    p.op("pe", lambda e: e.matmul(Pm[:], OnesR[:], RowB[:, 0:512],
                                                  start=True, stop=False),
                         wait=[eh1, evn.get(("done", th - 1))])
                    evm = p.op("pe", lambda e: e.matmul(Pm[:], OnesR[:], RowB[:, 512:1024],
                                                        start=False, stop=True),
                               wait=[el1], sem="PE")
                    p.op("pe", lambda e: e.matmul(Pr2[:], OnesR[:], RowB[:, 1024:1536],
                                                  start=True, stop=False), wait=[eh2])
                    evr = p.op("pe", lambda e: e.matmul(Pr2[:], OnesR[:], RowB[:, 1536:2048],
                                                        start=False, stop=True),
                               wait=[el2], sem="PE")
                    last = None
                    for oc in range(16):
                        xsl = X32[:, oc * NO + th * 512: oc * NO + (th + 1) * 512]
                        p.op("dve", (lambda xsl, th: lambda e: e.tensor_tensor(
                            SC32[th][:], xsl, Pm[:], ALU.subtract))(xsl, th),
                            wait=[evm, evr], sem="DVE")
                        e2 = p.op("dve", (lambda xsl, th: lambda e: e.tensor_tensor(
                            xsl, SC32[th][:], Pr2[:], ALU.mult))(xsl, th), sem="DVE")
                        evn[(oc, th)] = e2
                        last = e2
                        if make_xb:
                            evn[(oc, th)] = p.op("act", (lambda xsl, oc, th: lambda e: e.copy(
                                Xb[:, oc * NO + th * 512: oc * NO + (th + 1) * 512], xsl))(xsl, oc, th),
                                wait=[e2], sem="ACT")
                    evn[("done", th)] = last
                    last_dve_tail = last
            return evn

        with ExitStack() as ps_a3:
            Py = [ps_a3.enter_context(nc.psum_tensor(f"Py{i}", [128, 512], F32)) for i in range(2)]
            Pst = ps_a3.enter_context(nc.psum_tensor("Pst", [1, 4 * 512], F32))

            ev_wo = [None] * 16
            ev_wocons = [None] * 16

            def dma_wo(oc):
                war = ev_wocons[oc - 2] if oc >= 2 else ev_cons[-1]
                assert war is not None
                ev_wo[oc] = p.dma("qg", "AW", (lambda oc: lambda e: e.dma_start(WA[oc % 2][:], wo_d[oc]))(oc),
                                  wait=[war])

            dma_wo(0)
            dma_wo(1)
            ev_ev1 = {}
            ev_sq = {}
            for oc in range(16):
                for th in range(2):
                    yi = th
                    wwait = [ev_wo[oc], ev_norm[63] if oc == 0 and th == 0 else None,
                             ev_ev1.get((oc - 1, th))]
                    last = None
                    for dc in range(16):
                        rhs = AoT[:, dc * NO + th * 512: dc * NO + (th + 1) * 512]
                        last = p.op("pe", (lambda yi, dc, rhs, oc: lambda e: e.matmul(
                            Py[yi][:], WA[oc % 2][:, dc * 128:(dc + 1) * 128], rhs,
                            start=(dc == 0), stop=(dc == 15)))(yi, dc, rhs, oc),
                            wait=wwait if dc == 0 else [],
                            sem="PE" if dc == 15 else None)
                    if th == 1:
                        ev_wocons[oc] = last
                    xsl = X32[:, oc * NO + th * 512: oc * NO + (th + 1) * 512]
                    ev_ev1[(oc, th)] = p.op("dve", (lambda yi, xsl: lambda e: e.tensor_tensor(
                        xsl, Py[yi][:], xsl, ALU.add))(yi, xsl),
                        wait=[last, ev_x32], sem="DVE")
                    ev_b = p.op("act", (lambda xsl, th: lambda e: e.copy(SCB[th][:], xsl))(xsl, th),
                                wait=[ev_ev1[(oc, th)], ev_sq.get((oc - 1, th))], sem="ACT")
                    ev_q = p.op("act", (lambda xsl, th: lambda e: e.activation(
                        SQB[th][:], xsl, AF.Square))(xsl, th), sem="ACT")
                    p.op("pe", (lambda th, oc: lambda e: e.matmul(
                        Pst[:, th * 512:(th + 1) * 512], OnesB[:], SCB[th][:],
                        start=(oc == 0), stop=(oc == 15)))(th, oc),
                        wait=[ev_b])
                    ev_sq[(oc, th)] = p.op("pe", (lambda th, oc: lambda e: e.matmul(
                        Pst[:, (2 + th) * 512:(3 + th) * 512], OnesB[:], SQB[th][:],
                        start=(oc == 0), stop=(oc == 15)))(th, oc),
                        wait=[ev_q], sem="PE")
                if oc + 2 < 16:
                    dma_wo(oc + 2)
            ev_ln1 = emit_ln("1", ev_sq[(15, 1)], True, Pst, RowL, RowBL)

        st_ao.close()
        p.barrier()

        Wg = [sb(f"Wg{i}", [128, 2048], BF16) for i in range(2)]
        Wu = [sb(f"Wu{i}", [128, 2048], BF16) for i in range(2)]
        Wd = [sb(f"Wd{i}", [128, G * 2048], BF16) for i in range(2)]
        At = [sb(f"At{i}", [128, G * NO], BF16) for i in range(2)]
        Gs = [sb(f"Gs{i}", [128, 512], BF16) for i in range(2)]
        CsH = sb("CsH", [1, NO], BF16)
        CsL = sb("CsL", [1, NO], BF16)
        AsH = sb("AsH", [1, NO], BF16)
        AsL = sb("AsL", [1, NO], BF16)

        with ExitStack() as ps_b0:
            Pcs = ps_b0.enter_context(nc.psum_tensor("Pcs", [1, NO], F32))
            last = None
            for th in range(2):
                for dc in range(16):
                    last = p.op("pe", (lambda th, dc: lambda e: e.matmul(
                        Pcs[:, th * 512:(th + 1) * 512], OnesB[:],
                        Xb[:, dc * NO + th * 512: dc * NO + (th + 1) * 512],
                        start=(dc == 0), stop=(dc == 15)))(th, dc),
                        wait=[ev_ln1[(dc, th)]],
                        sem="PE" if dc == 15 else None)
            ev_csmm = last
            ev_csh = p.op("act", lambda e: e.copy(CsH[:], Pcs[:]), wait=[ev_csmm], sem="ACT")
            ev_csl = p.op("dve", lambda e: e.tensor_tensor(CsL[:], Pcs[:], CsH[:], ALU.subtract),
                          wait=[ev_csmm, ev_csh], sem="DVE")

        with ExitStack() as ps_b:
            Pas = ps_b.enter_context(nc.psum_tensor("Pas", [1, NO], F32))
            Pg = [ps_b.enter_context(nc.psum_tensor(f"Pg{i}", [128, 512], F32)) for i in range(2)]
            Pu = [ps_b.enter_context(nc.psum_tensor(f"Pu{i}", [128, 512], F32)) for i in range(2)]
            Pd = [ps_b.enter_context(nc.psum_tensor(f"Pd{i}", [128, 512], F32)) for i in range(2)]

            ev_fw = {}
            ev_gate = {}
            ev_up = {}
            ev_silu = {}
            ev_a = {}
            ev_asum = [None] * NH
            ev_dd = [None] * NG
            ev_evac = {}

            def dma_ffn(c):
                if c >= 2:
                    wg_war = ev_gate[(c - 2, 1)]
                    wu_war = ev_up[(c - 2, 1)]
                    assert wg_war is not None and wu_war is not None
                else:
                    wg_war = wu_war = ev_wocons[15]
                gprev = c // G - 2
                if gprev >= 0:
                    wd_war = ev_dd[gprev]
                    assert wd_war is not None
                else:
                    wd_war = ev_wocons[15]
                e1 = p.dma("qg", "FG", (lambda c: lambda e: e.dma_start(Wg[c % 2][:], wg_d[c]))(c),
                           wait=[wg_war])
                e2 = p.dma("qg", "FU", (lambda c: lambda e: e.dma_start(Wu[c % 2][:], wu_d[c]))(c),
                           wait=[wu_war])
                e3 = p.dma("qg", "FD", (lambda c: lambda e: e.dma_start(
                    Wd[(c // G) % 2][:, (c % G) * 2048:(c % G + 1) * 2048], wd_d[c]))(c),
                    wait=[wd_war])
                ev_fw[c] = (e1, e2, e3)

            dma_ffn(0)
            dma_ffn(1)

            def emit_asum(c):
                gi = (c // G) % 2
                sl = c % G
                for th in range(2):
                    ev_asum[c] = p.op("pe", (lambda gi, sl, th, c: lambda e: e.matmul(
                        Pas[:, th * 512:(th + 1) * 512], OnesB[:],
                        At[gi][:, sl * NO + th * 512: sl * NO + (th + 1) * 512],
                        start=(c == 0), stop=(c == NH - 1)))(gi, sl, th, c),
                        wait=[ev_a[(c, th)]], sem="PE")

            for c in range(NH):
                gg = c // G
                gi = gg % 2
                sl = c % G
                if c >= 2:
                    emit_asum(c - 2)
                for th in range(2):
                    bi = th
                    gw = [ev_fw[c][0], ev_csl, ev_csh, ev_silu.get((c - 1, th))]
                    for dc in range(16):
                        p.op("pe", (lambda bi, dc, c, th: lambda e: e.matmul(
                            Pg[bi][:], Wg[c % 2][:, dc * 128:(dc + 1) * 128],
                            Xb[:, dc * NO + th * 512: dc * NO + (th + 1) * 512],
                            start=(dc == 0), stop=False))(bi, dc, c, th),
                            wait=gw if dc == 0 else [])
                    p.op("pe", (lambda bi, th: lambda e: e.matmul(
                        Pg[bi][:], NegB[:], CsH[:, th * 512:(th + 1) * 512],
                        start=False, stop=False))(bi, th))
                    ev_gate[(c, th)] = p.op("pe", (lambda bi, th: lambda e: e.matmul(
                        Pg[bi][:], NegB[:], CsL[:, th * 512:(th + 1) * 512],
                        start=False, stop=True))(bi, th), sem="PE")
                    uw = [ev_fw[c][1], ev_a.get((c - 1, th))]
                    for dc in range(16):
                        p.op("pe", (lambda bi, dc, c, th: lambda e: e.matmul(
                            Pu[bi][:], Wu[c % 2][:, dc * 128:(dc + 1) * 128],
                            Xb[:, dc * NO + th * 512: dc * NO + (th + 1) * 512],
                            start=(dc == 0), stop=False))(bi, dc, c, th),
                            wait=uw if dc == 0 else [])
                    p.op("pe", (lambda bi, th: lambda e: e.matmul(
                        Pu[bi][:], NegB[:], CsH[:, th * 512:(th + 1) * 512],
                        start=False, stop=False))(bi, th))
                    ev_up[(c, th)] = p.op("pe", (lambda bi, th: lambda e: e.matmul(
                        Pu[bi][:], NegB[:], CsL[:, th * 512:(th + 1) * 512],
                        start=False, stop=True))(bi, th), sem="PE")
                    ev_silu[(c, th)] = p.op("act", (lambda bi, c, th: lambda e: e.activation(
                        Gs[th][:], Pg[bi][:], AF.Silu, scale=Sg[:, c:c + 1]))(bi, c, th),
                        wait=[ev_gate[(c, th)], ev_sg, ev_a.get((c - 1, th))], sem="ACT")
                    at_war = ev_dd[gg - 2] if gg >= 2 else None
                    at_war2 = ev_asum[c - 2 * G] if c >= 2 * G else None
                    ev_a[(c, th)] = p.op("dve", (lambda bi, gi, sl, c, th: lambda e: e.scalar_tensor_tensor(
                        At[gi][:, sl * NO + th * 512: sl * NO + (th + 1) * 512],
                        Pu[bi][:], Su[:, c:c + 1], Gs[th][:],
                        ALU.mult, ALU.mult))(bi, gi, sl, c, th),
                        wait=[ev_up[(c, th)], ev_silu[(c, th)], at_war, at_war2, ev_su], sem="DVE")
                if sl == G - 1:
                    lastg = (gg == NG - 1)
                    if lastg:
                        emit_asum(c - 1)
                        emit_asum(c)
                        ev_ash = p.op("act", lambda e: e.copy(AsH[:], Pas[:]),
                                      wait=[ev_asum[c]], sem="ACT")
                        ev_asl = p.op("dve", lambda e: e.tensor_tensor(AsL[:], Pas[:], AsH[:], ALU.subtract),
                                      wait=[ev_asum[c], ev_ash], sem="DVE")
                    for j in range(ND):
                        for th in range(2):
                            pj = th
                            dwait = [ev_a[(c, th)], ev_fw[c][2], ev_fw[c - 1][2],
                                     ev_evac.get((gg, j - 1, th))]
                            if j == 0 and gg >= 1:
                                dwait.append(ev_evac.get((gg - 1, ND - 1, th)))
                            rm = None
                            for s2 in range(G):
                                rm = p.op("pe", (lambda pj, gi, s2, j, th, lastg: lambda e: e.matmul(
                                    Pd[pj][:], Wd[gi][:, (s2 * ND + j) * 128:(s2 * ND + j + 1) * 128],
                                    At[gi][:, s2 * NO + th * 512: s2 * NO + (th + 1) * 512],
                                    start=(s2 == 0), stop=(s2 == G - 1) and not lastg))(pj, gi, s2, j, th, lastg),
                                    wait=dwait if s2 == 0 else [],
                                    sem="PE" if (s2 == G - 1 and not lastg) else None)
                            if lastg:
                                if j == 0 and th == 0:
                                    p.wait_only("pe", [ev_ash, ev_asl])
                                p.op("pe", (lambda pj, th: lambda e: e.matmul(
                                    Pd[pj][:], NegB[:], AsH[:, th * 512:(th + 1) * 512],
                                    start=False, stop=False))(pj, th))
                                rm = p.op("pe", (lambda pj, th: lambda e: e.matmul(
                                    Pd[pj][:], NegB[:], AsL[:, th * 512:(th + 1) * 512],
                                    start=False, stop=True))(pj, th), sem="PE")
                            xsl = X32[:, j * NO + th * 512: j * NO + (th + 1) * 512]
                            ev_evac[(gg, j, th)] = p.op("dve", (lambda pj, j, xsl: lambda e: e.scalar_tensor_tensor(
                                xsl, Pd[pj][:], Sd[:, j:j + 1], xsl, ALU.mult, ALU.add))(pj, j, xsl),
                                wait=[rm, ev_sd], sem="DVE")
                    ev_dd[gg] = ("PE", p.cnt["PE"])
                if c + 2 < NH:
                    dma_ffn(c + 2)

        p.barrier()

        with ExitStack() as ps_c:
            PstC = ps_c.enter_context(nc.psum_tensor("PstC", [1, 4 * 512], F32))
            ev_sq = {}
            for oc in range(16):
                for th in range(2):
                    xsl = X32[:, oc * NO + th * 512: oc * NO + (th + 1) * 512]
                    ev_b = p.op("act", (lambda xsl, th: lambda e: e.copy(SCB[th][:], xsl))(xsl, th),
                                wait=[ev_evac[(NG - 1, oc, th)], ev_sq.get((oc - 1, th))], sem="ACT")
                    ev_q = p.op("act", (lambda xsl, th: lambda e: e.activation(
                        SQB[th][:], xsl, AF.Square))(xsl, th), sem="ACT")
                    p.op("pe", (lambda th, oc: lambda e: e.matmul(
                        PstC[:, th * 512:(th + 1) * 512], OnesB[:], SCB[th][:],
                        start=(oc == 0), stop=(oc == 15)))(th, oc),
                        wait=[ev_b])
                    ev_sq[(oc, th)] = p.op("pe", (lambda th, oc: lambda e: e.matmul(
                        PstC[:, (2 + th) * 512:(3 + th) * 512], OnesB[:], SQB[th][:],
                        start=(oc == 0), stop=(oc == 15)))(th, oc),
                        wait=[ev_q], sem="PE")
            ev_ln2 = emit_ln("2", ev_sq[(15, 1)], False, PstC, RowL, RowBL)

        ev_os = []
        for oc in range(16):
            ev_os.append(p.dma("qs", "OS", (lambda oc: lambda e: e.dma_start(
                out_d[:, oc * NO:(oc + 1) * NO], X32[:, oc * NO:(oc + 1) * NO]))(oc),
                wait=[ev_ln2[(oc, 0)], ev_ln2[(oc, 1)]]))
        p.wait_only("qs", ev_os[-2:])

        block = st.enter_context(nc.Block())

        @block.tensor
        def _(t):
            for f in p.ops["pe"]:
                f(t)

        @block.scalar
        def _(s_):
            for f in p.ops["act"]:
                f(s_)

        @block.vector
        def _(v):
            for f in p.ops["dve"]:
                f(v)

        @block.gpsimd
        def _(g_):
            for f in p.ops["qg"]:
                f(g_)

        @block.sync
        def _(sy):
            for f in p.ops["qs"]:
                f(sy)
    return nc



def _feat_major(mat):
    T = mat.shape[0]
    return np.ascontiguousarray(mat.T.reshape(ND, 128, T).transpose(1, 0, 2).reshape(128, ND * T))


def _wblocks(w, nch_out):
    Din, F = w.shape
    ndc = Din // 128
    return np.ascontiguousarray(
        w.reshape(ndc, 128, nch_out, F // nch_out).transpose(2, 1, 0, 3)
        .reshape(nch_out, 128, ndc * (F // nch_out)))


def _rope_tables(positions):
    half = 32
    inv = (1.0 / THETA ** (np.arange(half, dtype=np.float32) / half)).astype(np.float32)
    pos = np.asarray(positions, dtype=np.float32)
    fr = np.outer(inv, pos).astype(np.float32)
    c = np.cos(fr).astype(np.float32)
    s = np.sin(fr).astype(np.float32)
    C = np.tile(np.concatenate([c, c], 0), (2, 1))
    Sm = np.tile(np.concatenate([-s, s], 0), (2, 1))
    return np.ascontiguousarray(C), np.ascontiguousarray(Sm)


_PROG_CACHE = {}


def kernel(L0_attn_x, L0_attn_Wq, L0_attn_Wk, L0_attn_Wv, L0_attn_Wo,
           L0_ffn_wq_gate, L0_ffn_s_gate, L0_ffn_wq_up, L0_ffn_s_up,
           L0_ffn_wq_down, L0_ffn_s_down):
    x = np.asarray(L0_attn_x, np.float32)
    Wq = np.asarray(L0_attn_Wq, np.float32)
    Wk = np.asarray(L0_attn_Wk, np.float32)
    Wv = np.asarray(L0_attn_Wv, np.float32)
    Wo = np.asarray(L0_attn_Wo, np.float32)
    wqg = np.ascontiguousarray(np.asarray(L0_ffn_wq_gate, np.int32))
    wqu = np.ascontiguousarray(np.asarray(L0_ffn_wq_up, np.int32))
    wqd = np.ascontiguousarray(np.asarray(L0_ffn_wq_down, np.int32))
    s_g = np.asarray(L0_ffn_s_gate, np.float32)
    s_u = np.asarray(L0_ffn_s_up, np.float32)
    s_d = np.asarray(L0_ffn_s_down, np.float32)

    perm_q = np.empty(D, np.int64)
    perm_o = np.empty(D, np.int64)
    for s_ in range(NHQ):
        h = SLOT_HEADS[s_]
        for j in range(64):
            perm_q[s_ * 64 + j] = h * 64 + ROPE_PERM[j]
            perm_o[s_ * 64 + j] = h * 64 + j
    perm_k = np.empty(NKV * 64, np.int64)
    for g in range(NKV):
        for j in range(64):
            perm_k[g * 64 + j] = g * 64 + ROPE_PERM[j]

    psw = np.zeros((128, 128), np.float32)
    for i in range(128):
        psw[i ^ 32, i] = 1.0

    consts = {
        "wq": _wblocks(Wq[:, perm_q], 16),
        "wk": _wblocks(Wk[:, perm_k], 4),
        "wv": np.ascontiguousarray(Wv.reshape(16, 128, 512)),
        "wo": _wblocks(Wo[perm_o, :], 16),
        "wg": np.ascontiguousarray(wqg.reshape(NH, 128, ND, 128).transpose(0, 3, 2, 1).reshape(NH, 128, ND * 128)),
        "wu": np.ascontiguousarray(wqu.reshape(NH, 128, ND, 128).transpose(0, 3, 2, 1).reshape(NH, 128, ND * 128)),
        "wd": np.ascontiguousarray(wqd.reshape(ND, 128, NH, 128).transpose(2, 3, 0, 1).reshape(NH, 128, ND * 128)),
        "sg": np.ascontiguousarray(s_g.reshape(NH, 128).T),
        "su": np.ascontiguousarray(s_u.reshape(NH, 128).T),
        "sd": np.ascontiguousarray(s_d.reshape(ND, 128).T),
        "onesb": np.ones((128, 1), np.float32),
        "negb": np.full((1, 128), -128.0, np.float32),
        "onesr": np.ones((1, 128), np.float32),
        "ones64": np.ones((1, 64), np.float32),
        "psw": psw,
    }

    in_maps = []
    own_all = []
    for core in range(8):
        b = core // 2
        half = core % 2
        blocks = _own_blocks(half)
        own = np.concatenate([np.arange(s0, s1) for s0, s1 in blocks])
        rest = np.setdiff1d(np.arange(S), own)
        key_order = np.concatenate([own, rest])
        own_all.append((b, own))
        x_dev = _feat_major(x[b][key_order])
        ck, sk = _rope_tables(key_order)
        m = np.zeros((128, 16 * 512), np.float32)
        tiles = [(0, kc) for kc in MASKABLE0] + [(1, kc) for kc in MASKABLE1]
        for t_i, (b_, kc) in enumerate(tiles):
            qpos = own[b_ * 512:(b_ + 1) * 512]
            kpos = key_order[kc * 128:(kc + 1) * 128]
            m[:, t_i * 512:(t_i + 1) * 512] = (kpos[:, None] <= qpos[None, :])
        in_maps.append({
            "x": x_dev, "ck": ck, "sk": sk,
            "msk": m, **consts,
        })

    if "nc" not in _PROG_CACHE:
        _PROG_CACHE["nc"] = build_program()
    nc = _PROG_CACHE["nc"]
    res = run_bass_kernel_spmd(nc, in_maps, core_ids=list(range(8)))

    out = np.empty((B, S, D), np.float32)
    for core in range(8):
        b, own = own_all[core]
        o = np.asarray(res.results[core]["out"], np.float32)
        o = o.reshape(128, ND, NO).transpose(1, 0, 2).reshape(D, NO)
        out[b, own, :] = o.T
    return out


# version 03
# speedup vs baseline: 1.2572x; 1.2572x over previous
import os
import numpy as np
import ml_dtypes
from contextlib import ExitStack

import concourse.bass as bass
import concourse.mybir as mybir
from concourse.bass_utils import run_bass_kernel_spmd

F32 = mybir.dt.float32
BF16 = mybir.dt.bfloat16
I32 = mybir.dt.int32
AF = mybir.ActivationFunctionType
ALU = mybir.AluOpType

B, S, D = 4, 2048, 2048
NHQ, NKV, HD = 32, 8, 64
HID = 5632
THETA = 10000.0
EPS = 1e-5
ND = D // 128
NH = HID // 128
NO = 1024
G = 4
NG = NH // G
EXPB = -24.0

_EVEN = [h for h in range(NHQ) if (h // 4) % 2 == 0]
_ODD = [h for h in range(NHQ) if (h // 4) % 2 == 1]
SLOT_HEADS = []
for _i in range(NHQ // 2):
    SLOT_HEADS.append(_EVEN[_i])
    SLOT_HEADS.append(_ODD[_i])
ROPE_PERM = [2 * j for j in range(32)] + [2 * j + 1 for j in range(32)]

CHUNKS0 = [0, 1, 2, 3, 8, 9, 10, 11]
CHUNKS1 = list(range(16))
MASKABLE0 = CHUNKS0
MASKABLE1 = [4, 5, 6, 7, 12, 13, 14, 15]


def _own_blocks(half):
    return [(0, 512), (1536, 2048)] if half == 0 else [(512, 1024), (1024, 1536)]


class Prog:
    DMA_FAMS = ("AW", "FG", "FU", "FD", "MG", "MS", "OS")

    def __init__(self, nc, st):
        self.nc = nc
        self.ops = {k: [] for k in ("pe", "act", "dve", "qg", "qs")}
        self.sems = {}
        self.cnt = {}
        self.waited = {}
        for s in ("PE", "ACT", "DVE"):
            self.sems[s] = st.enter_context(nc.semaphore(s))
            self.cnt[s] = 0
        self.fam_idx = {}
        for f in self.DMA_FAMS:
            for par in (0, 1):
                name = f + str(par)
                self.sems[name] = st.enter_context(nc.semaphore(name))
                self.cnt[name] = 0
            self.fam_idx[f] = 0

    def dma(self, eng, fam, fn, wait=()):
        idx = self.fam_idx[fam]
        self.fam_idx[fam] = idx + 1
        sname = fam + str(idx % 2)
        extra = []
        if idx >= 2:
            extra = [(sname, 16 * (idx // 2))]
        self.cnt[sname] += 16
        ev = (sname, self.cnt[sname])
        return self.op(eng, fn, wait=list(wait) + extra, sem=sname, n=16,
                       _preinc=True) or ev

    def op(self, eng, fn, wait=(), sem=None, n=1, _preinc=False):
        wait = list(wait)
        if eng == "dve":
            if sem is None:
                sem = "DVE"
            wait.append(("DVE", self.cnt["DVE"]))
        elif eng == "act":
            if sem is None:
                sem = "ACT"
            wait.append(("ACT", self.cnt["ACT"]))
        wa = []
        for evt in wait:
            if evt is None:
                continue
            s, c = evt
            k = (eng, s)
            if self.waited.get(k, 0) < c:
                self.waited[k] = c
                wa.append((self.sems[s], c))
        ev = None
        mark = None
        if sem is not None:
            if not _preinc:
                self.cnt[sem] += n
            ev = (sem, self.cnt[sem])
            mark = (self.sems[sem], n)

        def run(e, wa=wa, fn=fn, mark=mark):
            for h, c in wa:
                e.wait_ge(h, c)
            r = fn(e)
            if mark is not None:
                r.then_inc(*mark)
        self.ops[eng].append(run)
        return ev

    def wait_only(self, eng, wait):
        wa = []
        for evt in wait:
            if evt is None:
                continue
            s, c = evt
            k = (eng, s)
            if self.waited.get(k, 0) < c:
                self.waited[k] = c
                wa.append((self.sems[s], c))
        if wa:
            def run(e, wa=wa):
                for h, c in wa:
                    e.wait_ge(h, c)
            self.ops[eng].append(run)

    def barrier(self):
        evs = [(s, self.cnt[s]) for s in ("PE", "ACT", "DVE") if self.cnt[s] > 0]
        for eng in ("pe", "act", "dve"):
            self.wait_only(eng, evs)


def build_program():
    nc = bass.Bass("TRN2", target_bir_lowering=False, debug=False)
    x_d = nc.dram_tensor("x", [128, ND * S], F32, kind="ExternalInput").ap()
    wq_d = nc.dram_tensor("wq", [16, 128, 2048], F32, kind="ExternalInput").ap()
    wk_d = nc.dram_tensor("wk", [4, 128, 2048], F32, kind="ExternalInput").ap()
    wv_d = nc.dram_tensor("wv", [16, 128, 512], F32, kind="ExternalInput").ap()
    wo_d = nc.dram_tensor("wo", [16, 128, 2048], F32, kind="ExternalInput").ap()
    wg_d = nc.dram_tensor("wg", [NH, 128, 2048], I32, kind="ExternalInput").ap()
    wu_d = nc.dram_tensor("wu", [NH, 128, 2048], I32, kind="ExternalInput").ap()
    wd_d = nc.dram_tensor("wd", [NH, 128, 2048], I32, kind="ExternalInput").ap()
    sg_d = nc.dram_tensor("sg", [128, NH], F32, kind="ExternalInput").ap()
    su_d = nc.dram_tensor("su", [128, NH], F32, kind="ExternalInput").ap()
    sd_d = nc.dram_tensor("sd", [128, ND], F32, kind="ExternalInput").ap()
    onesb_d = nc.dram_tensor("onesb", [128, 1], F32, kind="ExternalInput").ap()
    negb_d = nc.dram_tensor("negb", [1, 128], F32, kind="ExternalInput").ap()
    onesr_d = nc.dram_tensor("onesr", [1, 128], F32, kind="ExternalInput").ap()
    ones64_d = nc.dram_tensor("ones64", [1, 64], F32, kind="ExternalInput").ap()
    psw_d = nc.dram_tensor("psw", [128, 128], F32, kind="ExternalInput").ap()
    ck_d = nc.dram_tensor("ck", [128, S], F32, kind="ExternalInput").ap()
    sk_d = nc.dram_tensor("sk", [128, S], F32, kind="ExternalInput").ap()
    msk_d = nc.dram_tensor("msk", [128, 16 * 512], F32, kind="ExternalInput").ap()
    out_d = nc.dram_tensor("out", [128, ND * NO], F32, kind="ExternalOutput").ap()

    st = ExitStack()
    with st:
        def sb(name, shape, dt):
            return st.enter_context(nc.sbuf_tensor(name, shape, dt))
        p = Prog(nc, st)

        OnesB = sb("OnesB", [128, 1], BF16)
        NegB = sb("NegB", [1, 128], BF16)
        OnesR = sb("OnesR", [1, 128], BF16)
        Ones64 = sb("Ones64", [1, 64], BF16)
        Sg = sb("Sg", [128, NH], F32)
        Su = sb("Su", [128, NH], F32)
        Sd = sb("Sd", [128, ND], F32)
        Xb = sb("Xb", [128, ND * NO], BF16)
        WA = [sb(f"WA{i}", [128, 2048], BF16) for i in range(2)]
        SC32 = [sb(f"SC32{i}", [128, 512], F32) for i in range(2)]
        SCB = [sb(f"SCB{i}", [128, 512], BF16) for i in range(2)]
        SQB = [sb(f"SQB{i}", [128, 512], BF16) for i in range(2)]

        st_qkv = ExitStack()
        qT = st_qkv.enter_context(nc.sbuf_tensor("qT", [128, 16 * NO], BF16))
        kT = st_qkv.enter_context(nc.sbuf_tensor("kT", [128, 4 * S], BF16))
        Vst = st_qkv.enter_context(nc.sbuf_tensor("Vst", [128, 16 * 520], BF16))

        st_a1 = ExitStack()
        Xf = st_a1.enter_context(nc.sbuf_tensor("Xf", [128, ND * S], BF16))
        Psw = st_a1.enter_context(nc.sbuf_tensor("Psw", [128, 128], BF16))
        Ck = st_a1.enter_context(nc.sbuf_tensor("Ck", [128, S], F32))
        Sk = st_a1.enter_context(nc.sbuf_tensor("Sk", [128, S], F32))

        p.dma("qg", "MG", lambda e: e.dma_start(OnesB[:], onesb_d))
        p.dma("qg", "MG", lambda e: e.dma_start(NegB[:], negb_d))
        p.dma("qg", "MG", lambda e: e.dma_start(OnesR[:], onesr_d))
        ev_cst = p.dma("qg", "MG", lambda e: e.dma_start(Ones64[:], ones64_d))
        ev_psw = p.dma("qg", "MG", lambda e: e.dma_start(Psw[:], psw_d))
        ev_xf = p.dma("qg", "MG", lambda e: e.dma_start(Xf[:], x_d))

        ev_sg = p.dma("qs", "MS", lambda e: e.dma_start(Sg[:], sg_d))
        ev_su = p.dma("qs", "MS", lambda e: e.dma_start(Su[:], su_d))
        ev_sd = p.dma("qs", "MS", lambda e: e.dma_start(Sd[:], sd_d))
        ev_ck = p.dma("qs", "MS", lambda e: e.dma_start(Ck[:], ck_d))
        ev_sk = p.dma("qs", "MS", lambda e: e.dma_start(Sk[:], sk_d))
        ev_vone = p.op("dve", lambda e: e.memset(
            Vst[:].rearrange("p (a g c) -> p a g c", g=8, c=65)[:, :, :, 64:65], 1.0),
            sem="DVE")

        items = [("q", i) for i in range(16)] + [("k", i) for i in range(4)] + \
                [("v", i % 16) for i in range(32)]
        ev_w = [None] * len(items)
        ev_cons = [None] * len(items)

        def dma_item(i):
            war = None
            if i >= 2:
                war = ev_cons[i - 2]
                assert war is not None, f"item {i} WAR missing"
            kind, idx = items[i]
            if kind == "q":
                src, dst = wq_d[idx], WA[i % 2][:]
            elif kind == "k":
                src, dst = wk_d[idx], WA[i % 2][:]
            else:
                src, dst = wv_d[idx], WA[i % 2][:, 0:512]
            ev_w[i] = p.dma("qg", "AW", (lambda src, dst: lambda e: e.dma_start(dst, src))(src, dst),
                            wait=[war])

        dma_item(0)
        dma_item(1)

        with ExitStack() as ps_a:
            Pq = [ps_a.enter_context(nc.psum_tensor(f"Pq{i}", [128, 512], F32)) for i in range(2)]
            Pw = [ps_a.enter_context(nc.psum_tensor(f"Pw{i}", [128, 512], F32)) for i in range(2)]

            steps = []
            for i, (kind, idx) in enumerate(items):
                if kind == "v":
                    continue
                nsub = 2 if kind == "q" else 4
                for sub in range(nsub):
                    steps.append((kind, i, idx, sub))

            ev_mm = [None] * len(steps)
            ev_cp = [None] * len(steps)
            ev_rp = [None] * len(steps)
            for r, (kind, i, idx, sub) in enumerate(steps):
                ri = r % 2
                first_wait = [ev_w[i], ev_xf, ev_rp[r - 2] if r >= 2 else None]
                for dc in range(16):
                    off = sub * 512 if kind == "q" else sub * 512
                    rhs = Xf[:, dc * S + off: dc * S + off + 512]
                    mm = p.op("pe", (lambda ri, dc, rhs, i: lambda e: e.matmul(
                        Pq[ri][:], WA[i % 2][:, dc * 128:(dc + 1) * 128], rhs,
                        start=(dc == 0), stop=(dc == 15)))(ri, dc, rhs, i),
                        wait=first_wait if dc == 0 else [],
                        sem="PE" if dc == 15 else None)
                ev_mm[r] = mm
                if sub == (1 if kind == "q" else 3):
                    ev_cons[i] = mm
                    if i + 2 < len(items) and items[i + 2][0] != "v":
                        dma_item(i + 2)
                ev_cp[r] = p.op("act", (lambda ri: lambda e: e.copy(SCB[ri][:], Pq[ri][:]))(ri),
                                wait=[ev_mm[r], ev_rp[r - 2] if r >= 2 else None], sem="ACT")
                ev_sw = p.op("pe", (lambda ri: lambda e: e.matmul(
                    Pw[ri][:], Psw[:], SCB[ri][:], start=True, stop=True))(ri),
                    wait=[ev_cp[r], ev_psw, ev_rp[r - 2] if r >= 2 else None], sem="PE")
                cab = Ck[:, sub * 512:(sub + 1) * 512]
                sab = Sk[:, sub * 512:(sub + 1) * 512]
                if kind == "q":
                    dst = qT[:, idx * NO + sub * 512: idx * NO + (sub + 1) * 512]
                else:
                    dst = kT[:, idx * S + sub * 512: idx * S + (sub + 1) * 512]
                p.op("dve", (lambda ri, cab: lambda e: e.tensor_tensor(
                    SC32[ri][:], Pq[ri][:], cab, ALU.mult))(ri, cab),
                    wait=[ev_mm[r], ev_cp[r], ev_ck, ev_sk], sem="DVE")
                p.op("dve", (lambda ri, sab: lambda e: e.tensor_tensor(
                    SCB[ri][:], Pw[ri][:], sab, ALU.mult))(ri, sab),
                    wait=[ev_sw], sem="DVE")
                ev_rp[r] = p.op("dve", (lambda ri, dst: lambda e: e.tensor_tensor(
                    dst, SC32[ri][:], SCB[ri][:], ALU.add))(ri, dst), sem="DVE")
            ev_rope_done = ev_rp[-1]

        dma_item(20)
        dma_item(21)
        with ExitStack() as ps_v:
            Pv = [ps_v.enter_context(nc.psum_tensor(f"Pv{i}", [128, 512], F32)) for i in range(8)]
            ev_vcp = {}
            vit = 20
            for pa in range(2):
                for dc in range(16):
                    i = vit + pa * 16 + dc
                    vw = [ev_w[i]]
                    if pa == 0 and dc == 0:
                        vw += [ev_xf, ev_rope_done]
                    if pa == 1 and dc == 0:
                        vw += [ev_vcp[kk2] for kk2 in range(8)]
                    last = None
                    for kk in range(8):
                        kc = pa * 8 + kk
                        last = p.op("pe", (lambda kk, dc, kc, i: lambda e: e.matmul(
                            Pv[kk][:], Xf[:, dc * S + kc * 128: dc * S + (kc + 1) * 128],
                            WA[i % 2][:, 0:512],
                            start=(dc == 0), stop=(dc == 15)))(kk, dc, kc, i),
                            wait=vw if kk == 0 else [],
                            sem="PE" if kk == 7 else None)
                    ev_cons[i] = last
                    if i + 2 < len(items):
                        dma_item(i + 2)
                for kk in range(8):
                    kc = pa * 8 + kk
                    dstv = Vst[:].rearrange("p (a g c) -> p a g c", g=8, c=65)[:, kc:kc + 1, :, 0:64]
                    srcv = Pv[kk][:].rearrange("p (a g c) -> p a g c", a=1, g=8, c=64)
                    ev_vcp[kc] = p.op("act", (lambda dstv, srcv: lambda e: e.copy(dstv, srcv))(dstv, srcv),
                                      wait=[ev_cons[vit + pa * 16 + 15], ev_vone], sem="ACT")

        st_a1.close()
        st_a2 = ExitStack()
        Msk = st_a2.enter_context(nc.sbuf_tensor("Msk", [128, 16 * 512], BF16))
        ExpT = st_a2.enter_context(nc.sbuf_tensor("ExpT", [128, 3 * 512], BF16))
        RB = [st_a2.enter_context(nc.sbuf_tensor(f"RB{i}", [128, 512], F32)) for i in range(2)]
        Row = st_a2.enter_context(nc.sbuf_tensor("Row", [1, 4 * 512], F32))
        RowB = st_a2.enter_context(nc.sbuf_tensor("RowB", [1, 4 * 512], BF16))
        st_ao = ExitStack()
        AoT = st_ao.enter_context(nc.sbuf_tensor("AoT", [128, 16 * NO], BF16, side="right"))
        ev_a1_pe = ("PE", p.cnt["PE"])
        ev_msk = p.dma("qg", "MG", lambda e: e.dma_start(Msk[:], msk_d), wait=[ev_a1_pe])

        p.barrier()

        with ExitStack() as ps_a2:
            Ps = [ps_a2.enter_context(nc.psum_tensor(f"Ps{i}", [128, 512], F32)) for i in range(2)]
            Po = [ps_a2.enter_context(nc.psum_tensor(f"Po{i}", [128, 512], F32)) for i in range(2)]
            Pden = [ps_a2.enter_context(nc.psum_tensor(f"Pden{i}", [1, 512], F32)) for i in range(2)]
            Pr = [ps_a2.enter_context(nc.psum_tensor(f"Pr{i}", [128, 512], F32)) for i in range(2)]

            ps_last_exp = {}
            exp_last_pv = {}
            ev_norm = [None] * 64
            ev_rbcp = [None] * 64
            for s_ in range(32):
                g = SLOT_HEADS[s_] // 4
                par = s_ % 2
                base = 64 * par
                for b_ in range(2):
                    it = s_ * 2 + b_
                    pi = it % 2
                    chunks = CHUNKS0 if b_ == 0 else CHUNKS1
                    mables = MASKABLE0 if b_ == 0 else MASKABLE1
                    nch = len(chunks)
                    ev_pvdone = None
                    pending = []

                    def emit_pv(ci, kc, ev_e):
                        nonlocal ev_pvdone
                        ei = ci % 3
                        pvwait = [ev_e]
                        if ci == 0:
                            pvwait.append(ev_norm[it - 2] if it >= 2 else None)
                        if par == 0:
                            lv = Vst[:, kc * 520 + g * 65: kc * 520 + (g + 1) * 65]
                            mmo = p.op("pe", (lambda lv, ei, pi, ci, nch: lambda e: e.matmul(
                                Po[pi][0:65, :], lv, ExpT[:, ei * 512:(ei + 1) * 512],
                                start=(ci == 0), stop=(ci == nch - 1)))(lv, ei, pi, ci, nch),
                                wait=pvwait, sem="PE")
                        else:
                            lv = Vst[:, kc * 520 + g * 65: kc * 520 + g * 65 + 64]
                            p.op("pe", (lambda lv, ei, pi, ci, nch: lambda e: e.matmul(
                                Po[pi][64:128, :], lv, ExpT[:, ei * 512:(ei + 1) * 512],
                                start=(ci == 0), stop=(ci == nch - 1)))(lv, ei, pi, ci, nch),
                                wait=pvwait)
                            mmo = p.op("pe", (lambda ei, pi, ci, nch: lambda e: e.matmul(
                                Pden[pi][:], OnesB[:], ExpT[:, ei * 512:(ei + 1) * 512],
                                start=(ci == 0), stop=(ci == nch - 1)))(ei, pi, ci, nch),
                                sem="PE")
                        exp_last_pv[ei] = mmo
                        ev_pvdone = mmo

                    for ci, kc in enumerate(chunks):
                        si = ci % 2
                        ei = ci % 3
                        lk = kT[base:base + 64, (g // 2) * S + kc * 128:(g // 2) * S + (kc + 1) * 128]
                        rq = qT[base:base + 64, (s_ // 2) * NO + b_ * 512:(s_ // 2) * NO + (b_ + 1) * 512]
                        ev_smm = p.op("pe", (lambda lk, rq, si: lambda e: e.matmul(
                            Ps[si][:], lk, rq, start=True, stop=True))(lk, rq, si),
                            wait=[ps_last_exp.get(si)], sem="PE")
                        ev_e = p.op("act", (lambda si, ei: lambda e: e.activation(
                            ExpT[:, ei * 512:(ei + 1) * 512], Ps[si][:], AF.Exp,
                            bias=np.float32(EXPB), scale=np.float32(0.125)))(si, ei),
                            wait=[ev_smm, exp_last_pv.get(ei)], sem="ACT")
                        ps_last_exp[si] = ev_e
                        if kc in mables:
                            mslot = MASKABLE0.index(kc) if b_ == 0 else 8 + MASKABLE1.index(kc)
                            ev_e = p.op("dve", (lambda ei, mslot: lambda e: e.tensor_tensor(
                                ExpT[:, ei * 512:(ei + 1) * 512], ExpT[:, ei * 512:(ei + 1) * 512],
                                Msk[:, mslot * 512:(mslot + 1) * 512], ALU.mult))(ei, mslot),
                                wait=[ev_e, ev_msk], sem="DVE")
                        pending.append((ci, kc, ev_e))
                        if len(pending) > 1:
                            emit_pv(*pending.pop(0))
                    emit_pv(*pending.pop(0))
                    den_src = Po[pi][64:65, :] if par == 0 else Pden[pi][0:1, :]
                    ev_dcp = p.op("dve", (lambda den_src, pi: lambda e: e.tensor_copy(
                        Row[:, pi * 512:(pi + 1) * 512], den_src))(den_src, pi),
                        wait=[ev_pvdone, ev_norm[it - 2] if it >= 2 else None], sem="DVE")
                    ev_rc = p.op("dve", (lambda pi: lambda e: e.reciprocal(
                        Row[:, (2 + pi) * 512:(3 + pi) * 512], Row[:, pi * 512:(pi + 1) * 512]))(pi),
                        wait=[ev_dcp], sem="DVE")
                    ev_rh = p.op("dve", (lambda pi: lambda e: e.tensor_copy(
                        RowB[:, pi * 512:(pi + 1) * 512], Row[:, (2 + pi) * 512:(3 + pi) * 512]))(pi),
                        wait=[ev_rc], sem="DVE")
                    ev_rl = p.op("dve", (lambda pi: lambda e: e.tensor_tensor(
                        RowB[:, (2 + pi) * 512:(3 + pi) * 512], Row[:, (2 + pi) * 512:(3 + pi) * 512],
                        RowB[:, pi * 512:(pi + 1) * 512], ALU.subtract))(pi),
                        wait=[ev_rh], sem="DVE")
                    p.op("pe", (lambda pi, base: lambda e: e.matmul(
                        Pr[pi][base:base + 64, :], Ones64[:], RowB[:, pi * 512:(pi + 1) * 512],
                        start=True, stop=False))(pi, base),
                        wait=[ev_rh, ev_cst, ev_rbcp[it - 2] if it >= 2 else None])
                    ev_bc = p.op("pe", (lambda pi, base: lambda e: e.matmul(
                        Pr[pi][base:base + 64, :], Ones64[:], RowB[:, (2 + pi) * 512:(3 + pi) * 512],
                        start=False, stop=True))(pi, base),
                        wait=[ev_rl], sem="PE")
                    ev_rbcp[it] = p.op("dve", (lambda pi, base: lambda e: e.tensor_copy(
                        RB[pi][base:base + 64, :], Pr[pi][base:base + 64, :]))(pi, base),
                        wait=[ev_bc], sem="DVE")
                    dsta = AoT[base:base + 64, (s_ // 2) * NO + b_ * 512:(s_ // 2) * NO + (b_ + 1) * 512]
                    srcp = Po[pi][0:64, :] if par == 0 else Po[pi][64:128, :]
                    ev_norm[it] = p.op("dve", (lambda dsta, srcp, pi, base: lambda e: e.tensor_tensor(
                        dsta, srcp, RB[pi][base:base + 64, :], ALU.mult))(dsta, srcp, pi, base),
                        wait=[ev_rbcp[it]], sem="DVE")

        ev_a2_pe = ("PE", p.cnt["PE"])
        st_a2.close()
        st_qkv.close()
        X32 = sb("X32", [128, ND * NO], F32)
        RowL = sb("RowL", [1, 4 * 512], F32)
        RowBL = sb("RowBL", [1, 4 * 512], BF16)
        ev_x32 = p.dma("qs", "MS", lambda e: e.dma_start(
            X32[:].rearrange("p (a t) -> p a t", t=NO),
            x_d[:].rearrange("p (a t) -> p a t", t=S)[:, :, 0:NO]),
            wait=[ev_a2_pe, ev_norm[63]])

        p.barrier()

        def emit_ln(tag, get_stats_ev, make_xb, Pst, Row, RowB):
            evn = {}
            with ExitStack() as pst2:
                Pm = pst2.enter_context(nc.psum_tensor("Pm" + tag, [128, 512], F32))
                Pr2 = pst2.enter_context(nc.psum_tensor("Pr2" + tag, [128, 512], F32))
                last_dve_tail = None
                last_act_tail = None
                for th in range(2):
                    ev1 = p.op("dve", (lambda th: lambda e: e.tensor_scalar(
                        Row[:, 0:512], Pst[:, th * 512:(th + 1) * 512],
                        1.0 / D, 0.0, ALU.mult, ALU.add))(th),
                        wait=[get_stats_ev, last_dve_tail, last_act_tail], sem="DVE")
                    p.op("dve", (lambda th: lambda e: e.tensor_scalar(
                        Row[:, 512:1024], Pst[:, (2 + th) * 512:(3 + th) * 512],
                        1.0 / D, 0.0, ALU.mult, ALU.add))(th), sem="DVE")
                    p.op("dve", lambda e: e.tensor_tensor(
                        Row[:, 1024:1536], Row[:, 0:512], Row[:, 0:512], ALU.mult), sem="DVE")
                    ev4 = p.op("dve", lambda e: e.tensor_tensor(
                        Row[:, 512:1024], Row[:, 512:1024], Row[:, 1024:1536], ALU.subtract), sem="DVE")
                    p.op("act", lambda e: e.activation(
                        Row[:, 1024:1536], Row[:, 512:1024], AF.Ln,
                        bias=np.float32(EPS), scale=np.float32(1.0)), wait=[ev4], sem="ACT")
                    ev6 = p.op("act", lambda e: e.activation(
                        Row[:, 1536:2048], Row[:, 1024:1536], AF.Exp,
                        bias=np.float32(0.0), scale=np.float32(-0.5)), sem="ACT")
                    eh1 = p.op("act", lambda e: e.copy(RowB[:, 0:512], Row[:, 0:512]),
                               wait=[ev1], sem="ACT")
                    el1 = p.op("dve", lambda e: e.tensor_tensor(
                        RowB[:, 512:1024], Row[:, 0:512], RowB[:, 0:512], ALU.subtract),
                        wait=[eh1], sem="DVE")
                    eh2 = p.op("act", lambda e: e.copy(RowB[:, 1024:1536], Row[:, 1536:2048]),
                               wait=[ev6], sem="ACT")
                    el2 = p.op("dve", lambda e: e.tensor_tensor(
                        RowB[:, 1536:2048], Row[:, 1536:2048], RowB[:, 1024:1536], ALU.subtract),
                        wait=[eh2], sem="DVE")
                    last_act_tail = eh2
                    p.op("pe", lambda e: e.matmul(Pm[:], OnesR[:], RowB[:, 0:512],
                                                  start=True, stop=False),
                         wait=[eh1, evn.get(("done", th - 1))])
                    evm = p.op("pe", lambda e: e.matmul(Pm[:], OnesR[:], RowB[:, 512:1024],
                                                        start=False, stop=True),
                               wait=[el1], sem="PE")
                    p.op("pe", lambda e: e.matmul(Pr2[:], OnesR[:], RowB[:, 1024:1536],
                                                  start=True, stop=False), wait=[eh2])
                    evr = p.op("pe", lambda e: e.matmul(Pr2[:], OnesR[:], RowB[:, 1536:2048],
                                                        start=False, stop=True),
                               wait=[el2], sem="PE")
                    last = None
                    for oc in range(16):
                        xsl = X32[:, oc * NO + th * 512: oc * NO + (th + 1) * 512]
                        p.op("dve", (lambda xsl, th: lambda e: e.tensor_tensor(
                            SC32[th][:], xsl, Pm[:], ALU.subtract))(xsl, th),
                            wait=[evm, evr], sem="DVE")
                        e2 = p.op("dve", (lambda xsl, th: lambda e: e.tensor_tensor(
                            xsl, SC32[th][:], Pr2[:], ALU.mult))(xsl, th), sem="DVE")
                        evn[(oc, th)] = e2
                        last = e2
                        if make_xb:
                            evn[(oc, th)] = p.op("act", (lambda xsl, oc, th: lambda e: e.copy(
                                Xb[:, oc * NO + th * 512: oc * NO + (th + 1) * 512], xsl))(xsl, oc, th),
                                wait=[e2], sem="ACT")
                    evn[("done", th)] = last
                    last_dve_tail = last
            return evn

        with ExitStack() as ps_a3:
            Py = [ps_a3.enter_context(nc.psum_tensor(f"Py{i}", [128, 512], F32)) for i in range(2)]
            Pst = ps_a3.enter_context(nc.psum_tensor("Pst", [1, 4 * 512], F32))

            ev_wo = [None] * 16
            ev_wocons = [None] * 16

            def dma_wo(oc):
                war = ev_wocons[oc - 2] if oc >= 2 else ev_cons[-1]
                assert war is not None
                ev_wo[oc] = p.dma("qg", "AW", (lambda oc: lambda e: e.dma_start(WA[oc % 2][:], wo_d[oc]))(oc),
                                  wait=[war])

            dma_wo(0)
            dma_wo(1)
            ev_ev1 = {}
            ev_sq = {}
            for oc in range(16):
                for th in range(2):
                    yi = th
                    wwait = [ev_wo[oc], ev_norm[63] if oc == 0 and th == 0 else None,
                             ev_ev1.get((oc - 1, th))]
                    last = None
                    for dc in range(16):
                        rhs = AoT[:, dc * NO + th * 512: dc * NO + (th + 1) * 512]
                        last = p.op("pe", (lambda yi, dc, rhs, oc: lambda e: e.matmul(
                            Py[yi][:], WA[oc % 2][:, dc * 128:(dc + 1) * 128], rhs,
                            start=(dc == 0), stop=(dc == 15)))(yi, dc, rhs, oc),
                            wait=wwait if dc == 0 else [],
                            sem="PE" if dc == 15 else None)
                    if th == 1:
                        ev_wocons[oc] = last
                    xsl = X32[:, oc * NO + th * 512: oc * NO + (th + 1) * 512]
                    ev_ev1[(oc, th)] = p.op("dve", (lambda yi, xsl: lambda e: e.tensor_tensor(
                        xsl, Py[yi][:], xsl, ALU.add))(yi, xsl),
                        wait=[last, ev_x32], sem="DVE")
                    ev_b = p.op("act", (lambda xsl, th: lambda e: e.copy(SCB[th][:], xsl))(xsl, th),
                                wait=[ev_ev1[(oc, th)], ev_sq.get((oc - 1, th))], sem="ACT")
                    ev_q = p.op("act", (lambda xsl, th: lambda e: e.activation(
                        SQB[th][:], xsl, AF.Square))(xsl, th), sem="ACT")
                    p.op("pe", (lambda th, oc: lambda e: e.matmul(
                        Pst[:, th * 512:(th + 1) * 512], OnesB[:], SCB[th][:],
                        start=(oc == 0), stop=(oc == 15)))(th, oc),
                        wait=[ev_b])
                    ev_sq[(oc, th)] = p.op("pe", (lambda th, oc: lambda e: e.matmul(
                        Pst[:, (2 + th) * 512:(3 + th) * 512], OnesB[:], SQB[th][:],
                        start=(oc == 0), stop=(oc == 15)))(th, oc),
                        wait=[ev_q], sem="PE")
                if oc + 2 < 16:
                    dma_wo(oc + 2)
            ev_ln1 = emit_ln("1", ev_sq[(15, 1)], True, Pst, RowL, RowBL)

        st_ao.close()
        p.barrier()

        Wg = [sb(f"Wg{i}", [128, 2048], BF16) for i in range(2)]
        Wu = [sb(f"Wu{i}", [128, 2048], BF16) for i in range(2)]
        Wd = [sb(f"Wd{i}", [128, G * 2048], BF16) for i in range(2)]
        At = [sb(f"At{i}", [128, G * NO], BF16) for i in range(2)]
        Gs = [sb(f"Gs{i}", [128, 512], BF16) for i in range(2)]
        CsH = sb("CsH", [1, NO], BF16)
        CsL = sb("CsL", [1, NO], BF16)
        AsH = sb("AsH", [1, NO], BF16)
        AsL = sb("AsL", [1, NO], BF16)

        with ExitStack() as ps_b0:
            Pcs = ps_b0.enter_context(nc.psum_tensor("Pcs", [1, NO], F32))
            last = None
            for th in range(2):
                for dc in range(16):
                    last = p.op("pe", (lambda th, dc: lambda e: e.matmul(
                        Pcs[:, th * 512:(th + 1) * 512], OnesB[:],
                        Xb[:, dc * NO + th * 512: dc * NO + (th + 1) * 512],
                        start=(dc == 0), stop=(dc == 15)))(th, dc),
                        wait=[ev_ln1[(dc, th)]],
                        sem="PE" if dc == 15 else None)
            ev_csmm = last
            ev_csh = p.op("act", lambda e: e.copy(CsH[:], Pcs[:]), wait=[ev_csmm], sem="ACT")
            ev_csl = p.op("dve", lambda e: e.tensor_tensor(CsL[:], Pcs[:], CsH[:], ALU.subtract),
                          wait=[ev_csmm, ev_csh], sem="DVE")

        with ExitStack() as ps_b:
            Pas = ps_b.enter_context(nc.psum_tensor("Pas", [1, NO], F32))
            Pg = [ps_b.enter_context(nc.psum_tensor(f"Pg{i}", [128, 512], F32)) for i in range(2)]
            Pu = [ps_b.enter_context(nc.psum_tensor(f"Pu{i}", [128, 512], F32)) for i in range(2)]
            Pd = [ps_b.enter_context(nc.psum_tensor(f"Pd{i}", [128, 512], F32)) for i in range(2)]

            ev_fw = {}
            ev_gate = {}
            ev_up = {}
            ev_silu = {}
            ev_a = {}
            ev_asum = [None] * NH
            ev_dd = [None] * NG
            ev_evac = {}

            def dma_ffn(c):
                if c >= 2:
                    wg_war = ev_gate[(c - 2, 1)]
                    wu_war = ev_up[(c - 2, 1)]
                    assert wg_war is not None and wu_war is not None
                else:
                    wg_war = wu_war = ev_wocons[15]
                gprev = c // G - 2
                if gprev >= 0:
                    wd_war = ev_dd[gprev]
                    assert wd_war is not None
                else:
                    wd_war = ev_wocons[15]
                e1 = p.dma("qg", "FG", (lambda c: lambda e: e.dma_start(Wg[c % 2][:], wg_d[c]))(c),
                           wait=[wg_war])
                e2 = p.dma("qg", "FU", (lambda c: lambda e: e.dma_start(Wu[c % 2][:], wu_d[c]))(c),
                           wait=[wu_war])
                e3 = p.dma("qg", "FD", (lambda c: lambda e: e.dma_start(
                    Wd[(c // G) % 2][:, (c % G) * 2048:(c % G + 1) * 2048], wd_d[c]))(c),
                    wait=[wd_war])
                ev_fw[c] = (e1, e2, e3)

            dma_ffn(0)
            dma_ffn(1)

            def emit_asum(c):
                gi = (c // G) % 2
                sl = c % G
                for th in range(2):
                    ev_asum[c] = p.op("pe", (lambda gi, sl, th, c: lambda e: e.matmul(
                        Pas[:, th * 512:(th + 1) * 512], OnesB[:],
                        At[gi][:, sl * NO + th * 512: sl * NO + (th + 1) * 512],
                        start=(c == 0), stop=(c == NH - 1)))(gi, sl, th, c),
                        wait=[ev_a[(c, th)]], sem="PE")

            for c in range(NH):
                gg = c // G
                gi = gg % 2
                sl = c % G
                if c >= 2:
                    emit_asum(c - 2)
                for th in range(2):
                    bi = th
                    gw = [ev_fw[c][0], ev_csl, ev_csh, ev_silu.get((c - 1, th))]
                    for dc in range(16):
                        p.op("pe", (lambda bi, dc, c, th: lambda e: e.matmul(
                            Pg[bi][:], Wg[c % 2][:, dc * 128:(dc + 1) * 128],
                            Xb[:, dc * NO + th * 512: dc * NO + (th + 1) * 512],
                            start=(dc == 0), stop=False))(bi, dc, c, th),
                            wait=gw if dc == 0 else [])
                    p.op("pe", (lambda bi, th: lambda e: e.matmul(
                        Pg[bi][:], NegB[:], CsH[:, th * 512:(th + 1) * 512],
                        start=False, stop=False))(bi, th))
                    ev_gate[(c, th)] = p.op("pe", (lambda bi, th: lambda e: e.matmul(
                        Pg[bi][:], NegB[:], CsL[:, th * 512:(th + 1) * 512],
                        start=False, stop=True))(bi, th), sem="PE")
                    uw = [ev_fw[c][1], ev_a.get((c - 1, th))]
                    for dc in range(16):
                        p.op("pe", (lambda bi, dc, c, th: lambda e: e.matmul(
                            Pu[bi][:], Wu[c % 2][:, dc * 128:(dc + 1) * 128],
                            Xb[:, dc * NO + th * 512: dc * NO + (th + 1) * 512],
                            start=(dc == 0), stop=False))(bi, dc, c, th),
                            wait=uw if dc == 0 else [])
                    p.op("pe", (lambda bi, th: lambda e: e.matmul(
                        Pu[bi][:], NegB[:], CsH[:, th * 512:(th + 1) * 512],
                        start=False, stop=False))(bi, th))
                    ev_up[(c, th)] = p.op("pe", (lambda bi, th: lambda e: e.matmul(
                        Pu[bi][:], NegB[:], CsL[:, th * 512:(th + 1) * 512],
                        start=False, stop=True))(bi, th), sem="PE")
                    ev_silu[(c, th)] = p.op("act", (lambda bi, c, th: lambda e: e.activation(
                        Gs[th][:], Pg[bi][:], AF.Silu, scale=Sg[:, c:c + 1]))(bi, c, th),
                        wait=[ev_gate[(c, th)], ev_sg, ev_a.get((c - 1, th))], sem="ACT")
                    at_war = ev_dd[gg - 2] if gg >= 2 else None
                    at_war2 = ev_asum[c - 2 * G] if c >= 2 * G else None
                    ev_a[(c, th)] = p.op("dve", (lambda bi, gi, sl, c, th: lambda e: e.scalar_tensor_tensor(
                        At[gi][:, sl * NO + th * 512: sl * NO + (th + 1) * 512],
                        Pu[bi][:], Su[:, c:c + 1], Gs[th][:],
                        ALU.mult, ALU.mult))(bi, gi, sl, c, th),
                        wait=[ev_up[(c, th)], ev_silu[(c, th)], at_war, at_war2, ev_su], sem="DVE")
                if sl == G - 1:
                    lastg = (gg == NG - 1)
                    if lastg:
                        emit_asum(c - 1)
                        emit_asum(c)
                        ev_ash = p.op("act", lambda e: e.copy(AsH[:], Pas[:]),
                                      wait=[ev_asum[c]], sem="ACT")
                        ev_asl = p.op("dve", lambda e: e.tensor_tensor(AsL[:], Pas[:], AsH[:], ALU.subtract),
                                      wait=[ev_asum[c], ev_ash], sem="DVE")
                    for j in range(ND):
                        for th in range(2):
                            pj = th
                            dwait = [ev_a[(c, th)], ev_fw[c][2], ev_fw[c - 1][2],
                                     ev_evac.get((gg, j - 1, th))]
                            if j == 0 and gg >= 1:
                                dwait.append(ev_evac.get((gg - 1, ND - 1, th)))
                            rm = None
                            for s2 in range(G):
                                rm = p.op("pe", (lambda pj, gi, s2, j, th, lastg: lambda e: e.matmul(
                                    Pd[pj][:], Wd[gi][:, (s2 * ND + j) * 128:(s2 * ND + j + 1) * 128],
                                    At[gi][:, s2 * NO + th * 512: s2 * NO + (th + 1) * 512],
                                    start=(s2 == 0), stop=(s2 == G - 1) and not lastg))(pj, gi, s2, j, th, lastg),
                                    wait=dwait if s2 == 0 else [],
                                    sem="PE" if (s2 == G - 1 and not lastg) else None)
                            if lastg:
                                if j == 0 and th == 0:
                                    p.wait_only("pe", [ev_ash, ev_asl])
                                p.op("pe", (lambda pj, th: lambda e: e.matmul(
                                    Pd[pj][:], NegB[:], AsH[:, th * 512:(th + 1) * 512],
                                    start=False, stop=False))(pj, th))
                                rm = p.op("pe", (lambda pj, th: lambda e: e.matmul(
                                    Pd[pj][:], NegB[:], AsL[:, th * 512:(th + 1) * 512],
                                    start=False, stop=True))(pj, th), sem="PE")
                            xsl = X32[:, j * NO + th * 512: j * NO + (th + 1) * 512]
                            ev_evac[(gg, j, th)] = p.op("dve", (lambda pj, j, xsl: lambda e: e.scalar_tensor_tensor(
                                xsl, Pd[pj][:], Sd[:, j:j + 1], xsl, ALU.mult, ALU.add))(pj, j, xsl),
                                wait=[rm, ev_sd], sem="DVE")
                    ev_dd[gg] = ("PE", p.cnt["PE"])
                if c + 2 < NH:
                    dma_ffn(c + 2)

        p.barrier()

        with ExitStack() as ps_c:
            PstC = ps_c.enter_context(nc.psum_tensor("PstC", [1, 4 * 512], F32))
            ev_sq = {}
            for oc in range(16):
                for th in range(2):
                    xsl = X32[:, oc * NO + th * 512: oc * NO + (th + 1) * 512]
                    ev_b = p.op("act", (lambda xsl, th: lambda e: e.copy(SCB[th][:], xsl))(xsl, th),
                                wait=[ev_evac[(NG - 1, oc, th)], ev_sq.get((oc - 1, th))], sem="ACT")
                    ev_q = p.op("act", (lambda xsl, th: lambda e: e.activation(
                        SQB[th][:], xsl, AF.Square))(xsl, th), sem="ACT")
                    p.op("pe", (lambda th, oc: lambda e: e.matmul(
                        PstC[:, th * 512:(th + 1) * 512], OnesB[:], SCB[th][:],
                        start=(oc == 0), stop=(oc == 15)))(th, oc),
                        wait=[ev_b])
                    ev_sq[(oc, th)] = p.op("pe", (lambda th, oc: lambda e: e.matmul(
                        PstC[:, (2 + th) * 512:(3 + th) * 512], OnesB[:], SQB[th][:],
                        start=(oc == 0), stop=(oc == 15)))(th, oc),
                        wait=[ev_q], sem="PE")
            ev_ln2 = emit_ln("2", ev_sq[(15, 1)], False, PstC, RowL, RowBL)

        ev_os = []
        for oc in range(16):
            ev_os.append(p.dma("qs", "OS", (lambda oc: lambda e: e.dma_start(
                out_d[:, oc * NO:(oc + 1) * NO], X32[:, oc * NO:(oc + 1) * NO]))(oc),
                wait=[ev_ln2[(oc, 0)], ev_ln2[(oc, 1)]]))
        p.wait_only("qs", ev_os[-2:])

        block = st.enter_context(nc.Block())

        @block.tensor
        def _(t):
            for f in p.ops["pe"]:
                f(t)

        @block.scalar
        def _(s_):
            for f in p.ops["act"]:
                f(s_)

        @block.vector
        def _(v):
            for f in p.ops["dve"]:
                f(v)

        @block.gpsimd
        def _(g_):
            for f in p.ops["qg"]:
                f(g_)

        @block.sync
        def _(sy):
            for f in p.ops["qs"]:
                f(sy)
    return nc



def _feat_major(mat):
    T = mat.shape[0]
    return np.ascontiguousarray(mat.T.reshape(ND, 128, T).transpose(1, 0, 2).reshape(128, ND * T))


def _wblocks(w, nch_out):
    Din, F = w.shape
    ndc = Din // 128
    return np.ascontiguousarray(
        w.reshape(ndc, 128, nch_out, F // nch_out).transpose(2, 1, 0, 3)
        .reshape(nch_out, 128, ndc * (F // nch_out)))


def _rope_tables(positions):
    half = 32
    inv = (1.0 / THETA ** (np.arange(half, dtype=np.float32) / half)).astype(np.float32)
    pos = np.asarray(positions, dtype=np.float32)
    fr = np.outer(inv, pos).astype(np.float32)
    c = np.cos(fr).astype(np.float32)
    s = np.sin(fr).astype(np.float32)
    C = np.tile(np.concatenate([c, c], 0), (2, 1))
    Sm = np.tile(np.concatenate([-s, s], 0), (2, 1))
    return np.ascontiguousarray(C), np.ascontiguousarray(Sm)


_PROG_CACHE = {}


def kernel(L0_attn_x, L0_attn_Wq, L0_attn_Wk, L0_attn_Wv, L0_attn_Wo,
           L0_ffn_wq_gate, L0_ffn_s_gate, L0_ffn_wq_up, L0_ffn_s_up,
           L0_ffn_wq_down, L0_ffn_s_down):
    x = np.asarray(L0_attn_x, np.float32)
    Wq = np.asarray(L0_attn_Wq, np.float32)
    Wk = np.asarray(L0_attn_Wk, np.float32)
    Wv = np.asarray(L0_attn_Wv, np.float32)
    Wo = np.asarray(L0_attn_Wo, np.float32)
    wqg = np.ascontiguousarray(np.asarray(L0_ffn_wq_gate, np.int32))
    wqu = np.ascontiguousarray(np.asarray(L0_ffn_wq_up, np.int32))
    wqd = np.ascontiguousarray(np.asarray(L0_ffn_wq_down, np.int32))
    s_g = np.asarray(L0_ffn_s_gate, np.float32)
    s_u = np.asarray(L0_ffn_s_up, np.float32)
    s_d = np.asarray(L0_ffn_s_down, np.float32)

    perm_q = np.empty(D, np.int64)
    perm_o = np.empty(D, np.int64)
    for s_ in range(NHQ):
        h = SLOT_HEADS[s_]
        for j in range(64):
            perm_q[s_ * 64 + j] = h * 64 + ROPE_PERM[j]
            perm_o[s_ * 64 + j] = h * 64 + j
    perm_k = np.empty(NKV * 64, np.int64)
    for g in range(NKV):
        for j in range(64):
            perm_k[g * 64 + j] = g * 64 + ROPE_PERM[j]

    psw = np.zeros((128, 128), np.float32)
    for i in range(128):
        psw[i ^ 32, i] = 1.0

    consts = {
        "wq": _wblocks(Wq[:, perm_q], 16),
        "wk": _wblocks(Wk[:, perm_k], 4),
        "wv": np.ascontiguousarray(Wv.reshape(16, 128, 512)),
        "wo": _wblocks(Wo[perm_o, :], 16),
        "wg": np.ascontiguousarray(wqg.reshape(NH, 128, ND, 128).transpose(0, 3, 2, 1).reshape(NH, 128, ND * 128)),
        "wu": np.ascontiguousarray(wqu.reshape(NH, 128, ND, 128).transpose(0, 3, 2, 1).reshape(NH, 128, ND * 128)),
        "wd": np.ascontiguousarray(wqd.reshape(ND, 128, NH, 128).transpose(2, 3, 0, 1).reshape(NH, 128, ND * 128)),
        "sg": np.ascontiguousarray(s_g.reshape(NH, 128).T),
        "su": np.ascontiguousarray(s_u.reshape(NH, 128).T),
        "sd": np.ascontiguousarray(s_d.reshape(ND, 128).T),
        "onesb": np.ones((128, 1), np.float32),
        "negb": np.full((1, 128), -128.0, np.float32),
        "onesr": np.ones((1, 128), np.float32),
        "ones64": np.ones((1, 64), np.float32),
        "psw": psw,
    }

    in_maps = []
    own_all = []
    for core in range(8):
        b = core // 2
        half = core % 2
        blocks = _own_blocks(half)
        own = np.concatenate([np.arange(s0, s1) for s0, s1 in blocks])
        rest = np.setdiff1d(np.arange(S), own)
        key_order = np.concatenate([own, rest])
        own_all.append((b, own))
        x_dev = _feat_major(x[b][key_order])
        ck, sk = _rope_tables(key_order)
        m = np.zeros((128, 16 * 512), np.float32)
        tiles = [(0, kc) for kc in MASKABLE0] + [(1, kc) for kc in MASKABLE1]
        for t_i, (b_, kc) in enumerate(tiles):
            qpos = own[b_ * 512:(b_ + 1) * 512]
            kpos = key_order[kc * 128:(kc + 1) * 128]
            m[:, t_i * 512:(t_i + 1) * 512] = (kpos[:, None] <= qpos[None, :])
        in_maps.append({
            "x": x_dev, "ck": ck, "sk": sk,
            "msk": m, **consts,
        })

    if "nc" not in _PROG_CACHE:
        _PROG_CACHE["nc"] = build_program()
    nc = _PROG_CACHE["nc"]
    res = run_bass_kernel_spmd(nc, in_maps, core_ids=list(range(8)))

    out = np.empty((B, S, D), np.float32)
    for core in range(8):
        b, own = own_all[core]
        o = np.asarray(res.results[core]["out"], np.float32)
        o = o.reshape(128, ND, NO).transpose(1, 0, 2).reshape(D, NO)
        out[b, own, :] = o.T
    return out


# version 04 (final)
# speedup vs baseline: 1.2822x; 1.0199x over previous
import os
import numpy as np
import ml_dtypes
from contextlib import ExitStack

import concourse.bass as bass
import concourse.mybir as mybir
from concourse.bass_utils import run_bass_kernel_spmd

F32 = mybir.dt.float32
BF16 = mybir.dt.bfloat16
I32 = mybir.dt.int32
AF = mybir.ActivationFunctionType
ALU = mybir.AluOpType

B, S, D = 4, 2048, 2048
NHQ, NKV, HD = 32, 8, 64
HID = 5632
THETA = 10000.0
EPS = 1e-5
ND = D // 128
NH = HID // 128
NO = 1024
G = 4
NG = NH // G
EXPB = -24.0

_EVEN = [h for h in range(NHQ) if (h // 4) % 2 == 0]
_ODD = [h for h in range(NHQ) if (h // 4) % 2 == 1]
SLOT_HEADS = []
for _i in range(NHQ // 2):
    SLOT_HEADS.append(_EVEN[_i])
    SLOT_HEADS.append(_ODD[_i])
ROPE_PERM = [2 * j for j in range(32)] + [2 * j + 1 for j in range(32)]

CHUNKS0 = [0, 1, 2, 3, 8, 9, 10, 11]
CHUNKS1 = list(range(16))
MASKABLE0 = CHUNKS0
MASKABLE1 = [4, 5, 6, 7, 12, 13, 14, 15]


def _own_blocks(half):
    return [(0, 512), (1536, 2048)] if half == 0 else [(512, 1024), (1024, 1536)]


class Prog:
    DMA_FAMS = ("AW", "FG", "FU", "FD", "MG", "MS", "OS")

    def __init__(self, nc, st):
        self.nc = nc
        self.ops = {k: [] for k in ("pe", "act", "dve", "qg", "qs")}
        self.sems = {}
        self.cnt = {}
        self.waited = {}
        for s in ("PE", "ACT", "DVE"):
            self.sems[s] = st.enter_context(nc.semaphore(s))
            self.cnt[s] = 0
        self.fam_idx = {}
        for f in self.DMA_FAMS:
            for par in (0, 1):
                name = f + str(par)
                self.sems[name] = st.enter_context(nc.semaphore(name))
                self.cnt[name] = 0
            self.fam_idx[f] = 0

    def dma(self, eng, fam, fn, wait=()):
        idx = self.fam_idx[fam]
        self.fam_idx[fam] = idx + 1
        sname = fam + str(idx % 2)
        extra = []
        if idx >= 2:
            extra = [(sname, 16 * (idx // 2))]
        self.cnt[sname] += 16
        ev = (sname, self.cnt[sname])
        return self.op(eng, fn, wait=list(wait) + extra, sem=sname, n=16,
                       _preinc=True) or ev

    def op(self, eng, fn, wait=(), sem=None, n=1, _preinc=False):
        wait = list(wait)
        if eng == "dve":
            if sem is None:
                sem = "DVE"
            wait.append(("DVE", self.cnt["DVE"]))
        elif eng == "act":
            if sem is None:
                sem = "ACT"
            wait.append(("ACT", self.cnt["ACT"]))
        wa = []
        for evt in wait:
            if evt is None:
                continue
            s, c = evt
            k = (eng, s)
            if self.waited.get(k, 0) < c:
                self.waited[k] = c
                wa.append((self.sems[s], c))
        ev = None
        mark = None
        if sem is not None:
            if not _preinc:
                self.cnt[sem] += n
            ev = (sem, self.cnt[sem])
            mark = (self.sems[sem], n)

        def run(e, wa=wa, fn=fn, mark=mark):
            for h, c in wa:
                e.wait_ge(h, c)
            r = fn(e)
            if mark is not None:
                r.then_inc(*mark)
        self.ops[eng].append(run)
        return ev

    def wait_only(self, eng, wait):
        wa = []
        for evt in wait:
            if evt is None:
                continue
            s, c = evt
            k = (eng, s)
            if self.waited.get(k, 0) < c:
                self.waited[k] = c
                wa.append((self.sems[s], c))
        if wa:
            def run(e, wa=wa):
                for h, c in wa:
                    e.wait_ge(h, c)
            self.ops[eng].append(run)

    def barrier(self):
        evs = [(s, self.cnt[s]) for s in ("PE", "ACT", "DVE") if self.cnt[s] > 0]
        for eng in ("pe", "act", "dve"):
            self.wait_only(eng, evs)


def build_program():
    nc = bass.Bass("TRN2", target_bir_lowering=False, debug=False)
    x_d = nc.dram_tensor("x", [128, ND * S], F32, kind="ExternalInput").ap()
    wq_d = nc.dram_tensor("wq", [16, 128, 2048], F32, kind="ExternalInput").ap()
    wk_d = nc.dram_tensor("wk", [4, 128, 2048], F32, kind="ExternalInput").ap()
    wv_d = nc.dram_tensor("wv", [16, 128, 512], F32, kind="ExternalInput").ap()
    wo_d = nc.dram_tensor("wo", [16, 128, 2048], F32, kind="ExternalInput").ap()
    wg_d = nc.dram_tensor("wg", [NH, 128, 2048], I32, kind="ExternalInput").ap()
    wu_d = nc.dram_tensor("wu", [NH, 128, 2048], I32, kind="ExternalInput").ap()
    wd_d = nc.dram_tensor("wd", [NH, 128, 2048], I32, kind="ExternalInput").ap()
    sg_d = nc.dram_tensor("sg", [128, NH], F32, kind="ExternalInput").ap()
    su_d = nc.dram_tensor("su", [128, NH], F32, kind="ExternalInput").ap()
    sd_d = nc.dram_tensor("sd", [128, ND], F32, kind="ExternalInput").ap()
    onesb_d = nc.dram_tensor("onesb", [128, 1], F32, kind="ExternalInput").ap()
    negb_d = nc.dram_tensor("negb", [1, 128], F32, kind="ExternalInput").ap()
    onesr_d = nc.dram_tensor("onesr", [1, 128], F32, kind="ExternalInput").ap()
    ones64_d = nc.dram_tensor("ones64", [1, 64], F32, kind="ExternalInput").ap()
    psw_d = nc.dram_tensor("psw", [128, 128], F32, kind="ExternalInput").ap()
    ck_d = nc.dram_tensor("ck", [128, S], F32, kind="ExternalInput").ap()
    sk_d = nc.dram_tensor("sk", [128, S], F32, kind="ExternalInput").ap()
    msk_d = nc.dram_tensor("msk", [128, 16 * 512], F32, kind="ExternalInput").ap()
    out_d = nc.dram_tensor("out", [128, ND * NO], F32, kind="ExternalOutput").ap()

    st = ExitStack()
    with st:
        def sb(name, shape, dt):
            return st.enter_context(nc.sbuf_tensor(name, shape, dt))
        p = Prog(nc, st)

        OnesB = sb("OnesB", [128, 1], BF16)
        NegB = sb("NegB", [1, 128], BF16)
        OnesR = sb("OnesR", [1, 128], BF16)
        Ones64 = sb("Ones64", [1, 64], BF16)
        Sg = sb("Sg", [128, NH], F32)
        Su = sb("Su", [128, NH], F32)
        Sd = sb("Sd", [128, ND], F32)
        Xb = sb("Xb", [128, ND * NO], BF16)
        WA = [sb(f"WA{i}", [128, 2048], BF16) for i in range(2)]
        SC32 = [sb(f"SC32{i}", [128, 512], F32) for i in range(2)]
        SCB = [sb(f"SCB{i}", [128, 512], BF16) for i in range(2)]
        SQB = [sb(f"SQB{i}", [128, 512], BF16) for i in range(2)]

        st_qkv = ExitStack()
        qT = st_qkv.enter_context(nc.sbuf_tensor("qT", [128, 16 * NO], BF16))
        kT = st_qkv.enter_context(nc.sbuf_tensor("kT", [128, 4 * S], BF16))
        Vst = st_qkv.enter_context(nc.sbuf_tensor("Vst", [128, 16 * 520], BF16))

        st_a1 = ExitStack()
        Xf = st_a1.enter_context(nc.sbuf_tensor("Xf", [128, ND * S], BF16))
        Psw = st_a1.enter_context(nc.sbuf_tensor("Psw", [128, 128], BF16))
        Ck = st_a1.enter_context(nc.sbuf_tensor("Ck", [128, S], F32))
        Sk = st_a1.enter_context(nc.sbuf_tensor("Sk", [128, S], F32))

        p.dma("qg", "MG", lambda e: e.dma_start(OnesB[:], onesb_d))
        p.dma("qg", "MG", lambda e: e.dma_start(NegB[:], negb_d))
        p.dma("qg", "MG", lambda e: e.dma_start(OnesR[:], onesr_d))
        ev_cst = p.dma("qg", "MG", lambda e: e.dma_start(Ones64[:], ones64_d))
        ev_psw = p.dma("qg", "MG", lambda e: e.dma_start(Psw[:], psw_d))
        ev_xf = p.dma("qg", "MG", lambda e: e.dma_start(Xf[:], x_d))

        ev_sg = p.dma("qs", "MS", lambda e: e.dma_start(Sg[:], sg_d))
        ev_su = p.dma("qs", "MS", lambda e: e.dma_start(Su[:], su_d))
        ev_sd = p.dma("qs", "MS", lambda e: e.dma_start(Sd[:], sd_d))
        ev_ck = p.dma("qs", "MS", lambda e: e.dma_start(Ck[:], ck_d))
        ev_sk = p.dma("qs", "MS", lambda e: e.dma_start(Sk[:], sk_d))
        ev_vone = p.op("dve", lambda e: e.memset(
            Vst[:].rearrange("p (a g c) -> p a g c", g=8, c=65)[:, :, :, 64:65], 1.0),
            sem="DVE")

        items = [("q", i) for i in range(16)] + [("k", i) for i in range(4)] + \
                [("v", i % 16) for i in range(32)]
        ev_w = [None] * len(items)
        ev_cons = [None] * len(items)

        def dma_item(i):
            war = None
            if i >= 2:
                war = ev_cons[i - 2]
                assert war is not None, f"item {i} WAR missing"
            kind, idx = items[i]
            if kind == "q":
                src, dst = wq_d[idx], WA[i % 2][:]
            elif kind == "k":
                src, dst = wk_d[idx], WA[i % 2][:]
            else:
                src, dst = wv_d[idx], WA[i % 2][:, 0:512]
            ev_w[i] = p.dma("qg", "AW", (lambda src, dst: lambda e: e.dma_start(dst, src))(src, dst),
                            wait=[war])

        dma_item(0)
        dma_item(1)

        with ExitStack() as ps_a:
            Pq = [ps_a.enter_context(nc.psum_tensor(f"Pq{i}", [128, 512], F32)) for i in range(2)]
            Pw = [ps_a.enter_context(nc.psum_tensor(f"Pw{i}", [128, 512], F32)) for i in range(2)]

            steps = []
            for i, (kind, idx) in enumerate(items):
                if kind == "v":
                    continue
                nsub = 2 if kind == "q" else 4
                for sub in range(nsub):
                    steps.append((kind, i, idx, sub))

            ev_mm = [None] * len(steps)
            ev_cp = [None] * len(steps)
            ev_rp = [None] * len(steps)
            for r, (kind, i, idx, sub) in enumerate(steps):
                ri = r % 2
                first_wait = [ev_w[i], ev_xf, ev_rp[r - 2] if r >= 2 else None]
                for dc in range(16):
                    off = sub * 512 if kind == "q" else sub * 512
                    rhs = Xf[:, dc * S + off: dc * S + off + 512]
                    mm = p.op("pe", (lambda ri, dc, rhs, i: lambda e: e.matmul(
                        Pq[ri][:], WA[i % 2][:, dc * 128:(dc + 1) * 128], rhs,
                        start=(dc == 0), stop=(dc == 15)))(ri, dc, rhs, i),
                        wait=first_wait if dc == 0 else [],
                        sem="PE" if dc == 15 else None)
                ev_mm[r] = mm
                if sub == (1 if kind == "q" else 3):
                    ev_cons[i] = mm
                    if i + 2 < len(items) and items[i + 2][0] != "v":
                        dma_item(i + 2)
                ev_cp[r] = p.op("act", (lambda ri: lambda e: e.copy(SCB[ri][:], Pq[ri][:]))(ri),
                                wait=[ev_mm[r], ev_rp[r - 2] if r >= 2 else None], sem="ACT")
                ev_sw = p.op("pe", (lambda ri: lambda e: e.matmul(
                    Pw[ri][:], Psw[:], SCB[ri][:], start=True, stop=True))(ri),
                    wait=[ev_cp[r], ev_psw, ev_rp[r - 2] if r >= 2 else None], sem="PE")
                cab = Ck[:, sub * 512:(sub + 1) * 512]
                sab = Sk[:, sub * 512:(sub + 1) * 512]
                if kind == "q":
                    dst = qT[:, idx * NO + sub * 512: idx * NO + (sub + 1) * 512]
                else:
                    dst = kT[:, idx * S + sub * 512: idx * S + (sub + 1) * 512]
                p.op("dve", (lambda ri, cab: lambda e: e.tensor_tensor(
                    SC32[ri][:], Pq[ri][:], cab, ALU.mult))(ri, cab),
                    wait=[ev_mm[r], ev_cp[r], ev_ck, ev_sk], sem="DVE")
                p.op("dve", (lambda ri, sab: lambda e: e.tensor_tensor(
                    SCB[ri][:], Pw[ri][:], sab, ALU.mult))(ri, sab),
                    wait=[ev_sw], sem="DVE")
                ev_rp[r] = p.op("dve", (lambda ri, dst: lambda e: e.tensor_tensor(
                    dst, SC32[ri][:], SCB[ri][:], ALU.add))(ri, dst), sem="DVE")
            ev_rope_done = ev_rp[-1]

        dma_item(20)
        dma_item(21)
        with ExitStack() as ps_v:
            Pv = [ps_v.enter_context(nc.psum_tensor(f"Pv{i}", [128, 512], F32)) for i in range(8)]
            ev_vcp = {}
            vit = 20
            for pa in range(2):
                for dc in range(16):
                    i = vit + pa * 16 + dc
                    vw = [ev_w[i]]
                    if pa == 0 and dc == 0:
                        vw += [ev_xf, ev_rope_done]
                    if pa == 1 and dc == 0:
                        vw += [ev_vcp[kk2] for kk2 in range(8)]
                    last = None
                    for kk in range(8):
                        kc = pa * 8 + kk
                        last = p.op("pe", (lambda kk, dc, kc, i: lambda e: e.matmul(
                            Pv[kk][:], Xf[:, dc * S + kc * 128: dc * S + (kc + 1) * 128],
                            WA[i % 2][:, 0:512],
                            start=(dc == 0), stop=(dc == 15)))(kk, dc, kc, i),
                            wait=vw if kk == 0 else [],
                            sem="PE" if kk == 7 else None)
                    ev_cons[i] = last
                    if i + 2 < len(items):
                        dma_item(i + 2)
                for kk in range(8):
                    kc = pa * 8 + kk
                    dstv = Vst[:].rearrange("p (a g c) -> p a g c", g=8, c=65)[:, kc:kc + 1, :, 0:64]
                    srcv = Pv[kk][:].rearrange("p (a g c) -> p a g c", a=1, g=8, c=64)
                    ev_vcp[kc] = p.op("act", (lambda dstv, srcv: lambda e: e.copy(dstv, srcv))(dstv, srcv),
                                      wait=[ev_cons[vit + pa * 16 + 15], ev_vone], sem="ACT")

        st_a1.close()
        st_a2 = ExitStack()
        Msk = st_a2.enter_context(nc.sbuf_tensor("Msk", [128, 16 * 512], BF16))
        ExpT = st_a2.enter_context(nc.sbuf_tensor("ExpT", [128, 3 * 512], BF16))
        RB = [st_a2.enter_context(nc.sbuf_tensor(f"RB{i}", [128, 512], F32)) for i in range(2)]
        Row = st_a2.enter_context(nc.sbuf_tensor("Row", [1, 4 * 512], F32))
        RowB = st_a2.enter_context(nc.sbuf_tensor("RowB", [1, 4 * 512], BF16))
        st_ao = ExitStack()
        AoT = st_ao.enter_context(nc.sbuf_tensor("AoT", [128, 16 * NO], BF16, side="right"))
        ev_a1_pe = ("PE", p.cnt["PE"])
        ev_msk = p.dma("qg", "MG", lambda e: e.dma_start(Msk[:], msk_d), wait=[ev_a1_pe])

        p.barrier()

        with ExitStack() as ps_a2:
            Ps = [ps_a2.enter_context(nc.psum_tensor(f"Ps{i}", [128, 512], F32)) for i in range(2)]
            Po = [ps_a2.enter_context(nc.psum_tensor(f"Po{i}", [128, 512], F32)) for i in range(2)]
            Pden = [ps_a2.enter_context(nc.psum_tensor(f"Pden{i}", [1, 512], F32)) for i in range(2)]
            Pr = [ps_a2.enter_context(nc.psum_tensor(f"Pr{i}", [128, 512], F32)) for i in range(2)]

            ps_last_exp = {}
            exp_last_pv = {}
            ev_norm = [None] * 64
            ev_rbcp = [None] * 64
            for s_ in range(32):
                g = SLOT_HEADS[s_] // 4
                par = s_ % 2
                base = 64 * par
                for b_ in range(2):
                    it = s_ * 2 + b_
                    pi = it % 2
                    chunks = CHUNKS0 if b_ == 0 else CHUNKS1
                    mables = MASKABLE0 if b_ == 0 else MASKABLE1
                    nch = len(chunks)
                    ev_pvdone = None
                    pending = []

                    def emit_pv(ci, kc, ev_e):
                        nonlocal ev_pvdone
                        ei = ci % 3
                        pvwait = [ev_e]
                        if ci == 0:
                            pvwait.append(ev_norm[it - 2] if it >= 2 else None)
                        if par == 0:
                            lv = Vst[:, kc * 520 + g * 65: kc * 520 + (g + 1) * 65]
                            mmo = p.op("pe", (lambda lv, ei, pi, ci, nch: lambda e: e.matmul(
                                Po[pi][0:65, :], lv, ExpT[:, ei * 512:(ei + 1) * 512],
                                start=(ci == 0), stop=(ci == nch - 1)))(lv, ei, pi, ci, nch),
                                wait=pvwait, sem="PE")
                        else:
                            lv = Vst[:, kc * 520 + g * 65: kc * 520 + g * 65 + 64]
                            p.op("pe", (lambda lv, ei, pi, ci, nch: lambda e: e.matmul(
                                Po[pi][64:128, :], lv, ExpT[:, ei * 512:(ei + 1) * 512],
                                start=(ci == 0), stop=(ci == nch - 1)))(lv, ei, pi, ci, nch),
                                wait=pvwait)
                            mmo = p.op("pe", (lambda ei, pi, ci, nch: lambda e: e.matmul(
                                Pden[pi][:], OnesB[:], ExpT[:, ei * 512:(ei + 1) * 512],
                                start=(ci == 0), stop=(ci == nch - 1)))(ei, pi, ci, nch),
                                sem="PE")
                        exp_last_pv[ei] = mmo
                        ev_pvdone = mmo

                    for ci, kc in enumerate(chunks):
                        si = ci % 2
                        ei = ci % 3
                        lk = kT[base:base + 64, (g // 2) * S + kc * 128:(g // 2) * S + (kc + 1) * 128]
                        rq = qT[base:base + 64, (s_ // 2) * NO + b_ * 512:(s_ // 2) * NO + (b_ + 1) * 512]
                        ev_smm = p.op("pe", (lambda lk, rq, si: lambda e: e.matmul(
                            Ps[si][:], lk, rq, start=True, stop=True))(lk, rq, si),
                            wait=[ps_last_exp.get(si)], sem="PE")
                        ev_e = p.op("act", (lambda si, ei: lambda e: e.activation(
                            ExpT[:, ei * 512:(ei + 1) * 512], Ps[si][:], AF.Exp,
                            bias=np.float32(EXPB), scale=np.float32(0.125)))(si, ei),
                            wait=[ev_smm, exp_last_pv.get(ei)], sem="ACT")
                        ps_last_exp[si] = ev_e
                        if kc in mables:
                            mslot = MASKABLE0.index(kc) if b_ == 0 else 8 + MASKABLE1.index(kc)
                            ev_e = p.op("dve", (lambda ei, mslot: lambda e: e.tensor_tensor(
                                ExpT[:, ei * 512:(ei + 1) * 512], ExpT[:, ei * 512:(ei + 1) * 512],
                                Msk[:, mslot * 512:(mslot + 1) * 512], ALU.mult))(ei, mslot),
                                wait=[ev_e, ev_msk], sem="DVE")
                        pending.append((ci, kc, ev_e))
                        if len(pending) > 1:
                            emit_pv(*pending.pop(0))
                    emit_pv(*pending.pop(0))
                    den_src = Po[pi][64:65, :] if par == 0 else Pden[pi][0:1, :]
                    ev_dcp = p.op("dve", (lambda den_src, pi: lambda e: e.tensor_copy(
                        Row[:, pi * 512:(pi + 1) * 512], den_src))(den_src, pi),
                        wait=[ev_pvdone, ev_norm[it - 2] if it >= 2 else None], sem="DVE")
                    ev_rc = p.op("dve", (lambda pi: lambda e: e.reciprocal(
                        Row[:, (2 + pi) * 512:(3 + pi) * 512], Row[:, pi * 512:(pi + 1) * 512]))(pi),
                        wait=[ev_dcp], sem="DVE")
                    ev_rh = p.op("dve", (lambda pi: lambda e: e.tensor_copy(
                        RowB[:, pi * 512:(pi + 1) * 512], Row[:, (2 + pi) * 512:(3 + pi) * 512]))(pi),
                        wait=[ev_rc], sem="DVE")
                    ev_rl = p.op("dve", (lambda pi: lambda e: e.tensor_tensor(
                        RowB[:, (2 + pi) * 512:(3 + pi) * 512], Row[:, (2 + pi) * 512:(3 + pi) * 512],
                        RowB[:, pi * 512:(pi + 1) * 512], ALU.subtract))(pi),
                        wait=[ev_rh], sem="DVE")
                    p.op("pe", (lambda pi, base: lambda e: e.matmul(
                        Pr[pi][base:base + 64, :], Ones64[:], RowB[:, pi * 512:(pi + 1) * 512],
                        start=True, stop=False))(pi, base),
                        wait=[ev_rh, ev_cst, ev_rbcp[it - 2] if it >= 2 else None])
                    ev_bc = p.op("pe", (lambda pi, base: lambda e: e.matmul(
                        Pr[pi][base:base + 64, :], Ones64[:], RowB[:, (2 + pi) * 512:(3 + pi) * 512],
                        start=False, stop=True))(pi, base),
                        wait=[ev_rl], sem="PE")
                    ev_rbcp[it] = p.op("dve", (lambda pi, base: lambda e: e.tensor_copy(
                        RB[pi][base:base + 64, :], Pr[pi][base:base + 64, :]))(pi, base),
                        wait=[ev_bc], sem="DVE")
                    dsta = AoT[base:base + 64, (s_ // 2) * NO + b_ * 512:(s_ // 2) * NO + (b_ + 1) * 512]
                    srcp = Po[pi][0:64, :] if par == 0 else Po[pi][64:128, :]
                    ev_norm[it] = p.op("dve", (lambda dsta, srcp, pi, base: lambda e: e.tensor_tensor(
                        dsta, srcp, RB[pi][base:base + 64, :], ALU.mult))(dsta, srcp, pi, base),
                        wait=[ev_rbcp[it]], sem="DVE")

        ev_a2_pe = ("PE", p.cnt["PE"])
        st_a2.close()
        st_qkv.close()
        X32 = sb("X32", [128, ND * NO], F32)
        RowL = sb("RowL", [1, 4 * 512], F32)
        RowBL = sb("RowBL", [1, 4 * 512], BF16)
        ev_x32 = p.dma("qs", "MS", lambda e: e.dma_start(
            X32[:].rearrange("p (a t) -> p a t", t=NO),
            x_d[:].rearrange("p (a t) -> p a t", t=S)[:, :, 0:NO]),
            wait=[ev_a2_pe, ev_norm[63]])

        p.barrier()

        def emit_ln(tag, get_stats_ev, make_xb, Pst, Row, RowB):
            evn = {}
            with ExitStack() as pst2:
                Pm = pst2.enter_context(nc.psum_tensor("Pm" + tag, [128, 512], F32))
                Pr2 = pst2.enter_context(nc.psum_tensor("Pr2" + tag, [128, 512], F32))
                last_dve_tail = None
                last_act_tail = None
                for th in range(2):
                    ev1 = p.op("dve", (lambda th: lambda e: e.tensor_scalar(
                        Row[:, 0:512], Pst[:, th * 512:(th + 1) * 512],
                        1.0 / D, 0.0, ALU.mult, ALU.add))(th),
                        wait=[get_stats_ev, last_dve_tail, last_act_tail], sem="DVE")
                    p.op("dve", (lambda th: lambda e: e.tensor_scalar(
                        Row[:, 512:1024], Pst[:, (2 + th) * 512:(3 + th) * 512],
                        1.0 / D, 0.0, ALU.mult, ALU.add))(th), sem="DVE")
                    p.op("dve", lambda e: e.tensor_tensor(
                        Row[:, 1024:1536], Row[:, 0:512], Row[:, 0:512], ALU.mult), sem="DVE")
                    ev4 = p.op("dve", lambda e: e.tensor_tensor(
                        Row[:, 512:1024], Row[:, 512:1024], Row[:, 1024:1536], ALU.subtract), sem="DVE")
                    p.op("act", lambda e: e.activation(
                        Row[:, 1024:1536], Row[:, 512:1024], AF.Ln,
                        bias=np.float32(EPS), scale=np.float32(1.0)), wait=[ev4], sem="ACT")
                    ev6 = p.op("act", lambda e: e.activation(
                        Row[:, 1536:2048], Row[:, 1024:1536], AF.Exp,
                        bias=np.float32(0.0), scale=np.float32(-0.5)), sem="ACT")
                    eh1 = p.op("act", lambda e: e.copy(RowB[:, 0:512], Row[:, 0:512]),
                               wait=[ev1], sem="ACT")
                    el1 = p.op("dve", lambda e: e.tensor_tensor(
                        RowB[:, 512:1024], Row[:, 0:512], RowB[:, 0:512], ALU.subtract),
                        wait=[eh1], sem="DVE")
                    eh2 = p.op("act", lambda e: e.copy(RowB[:, 1024:1536], Row[:, 1536:2048]),
                               wait=[ev6], sem="ACT")
                    el2 = p.op("dve", lambda e: e.tensor_tensor(
                        RowB[:, 1536:2048], Row[:, 1536:2048], RowB[:, 1024:1536], ALU.subtract),
                        wait=[eh2], sem="DVE")
                    last_act_tail = eh2
                    p.op("pe", lambda e: e.matmul(Pm[:], OnesR[:], RowB[:, 0:512],
                                                  start=True, stop=False),
                         wait=[eh1, evn.get(("done", th - 1))])
                    evm = p.op("pe", lambda e: e.matmul(Pm[:], OnesR[:], RowB[:, 512:1024],
                                                        start=False, stop=True),
                               wait=[el1], sem="PE")
                    p.op("pe", lambda e: e.matmul(Pr2[:], OnesR[:], RowB[:, 1024:1536],
                                                  start=True, stop=False), wait=[eh2])
                    evr = p.op("pe", lambda e: e.matmul(Pr2[:], OnesR[:], RowB[:, 1536:2048],
                                                        start=False, stop=True),
                               wait=[el2], sem="PE")
                    last = None
                    for oc in range(16):
                        xsl = X32[:, oc * NO + th * 512: oc * NO + (th + 1) * 512]
                        p.op("dve", (lambda xsl, th: lambda e: e.tensor_tensor(
                            SC32[th][:], xsl, Pm[:], ALU.subtract))(xsl, th),
                            wait=[evm, evr], sem="DVE")
                        e2 = p.op("dve", (lambda xsl, th: lambda e: e.tensor_tensor(
                            xsl, SC32[th][:], Pr2[:], ALU.mult))(xsl, th), sem="DVE")
                        evn[(oc, th)] = e2
                        last = e2
                        if make_xb:
                            evn[(oc, th)] = p.op("act", (lambda xsl, oc, th: lambda e: e.copy(
                                Xb[:, oc * NO + th * 512: oc * NO + (th + 1) * 512], xsl))(xsl, oc, th),
                                wait=[e2], sem="ACT")
                    evn[("done", th)] = last
                    last_dve_tail = last
            return evn

        with ExitStack() as ps_a3:
            Py = [ps_a3.enter_context(nc.psum_tensor(f"Py{i}", [128, 512], F32)) for i in range(2)]
            Pst = ps_a3.enter_context(nc.psum_tensor("Pst", [1, 4 * 512], F32))

            ev_wo = [None] * 16
            ev_wocons = [None] * 16

            def dma_wo(oc):
                war = ev_wocons[oc - 2] if oc >= 2 else ev_cons[-1]
                assert war is not None
                ev_wo[oc] = p.dma("qg", "AW", (lambda oc: lambda e: e.dma_start(WA[oc % 2][:], wo_d[oc]))(oc),
                                  wait=[war])

            dma_wo(0)
            dma_wo(1)
            ev_ev1 = {}
            ev_sq = {}
            pend_st = []

            def flush_stats():
                th2, oc2, evb2, evq2 = pend_st.pop(0)
                p.op("pe", (lambda th, oc: lambda e: e.matmul(
                    Pst[:, th * 512:(th + 1) * 512], OnesB[:], SCB[th][:],
                    start=(oc == 0), stop=(oc == 15)))(th2, oc2),
                    wait=[evb2])
                ev_sq[(oc2, th2)] = p.op("pe", (lambda th, oc: lambda e: e.matmul(
                    Pst[:, (2 + th) * 512:(3 + th) * 512], OnesB[:], SQB[th][:],
                    start=(oc == 0), stop=(oc == 15)))(th2, oc2),
                    wait=[evq2], sem="PE")

            for oc in range(16):
                for th in range(2):
                    if len(pend_st) >= 2:
                        flush_stats()
                    yi = th
                    wwait = [ev_wo[oc], ev_norm[63] if oc == 0 and th == 0 else None,
                             ev_ev1.get((oc - 1, th))]
                    last = None
                    for dc in range(16):
                        rhs = AoT[:, dc * NO + th * 512: dc * NO + (th + 1) * 512]
                        last = p.op("pe", (lambda yi, dc, rhs, oc: lambda e: e.matmul(
                            Py[yi][:], WA[oc % 2][:, dc * 128:(dc + 1) * 128], rhs,
                            start=(dc == 0), stop=(dc == 15)))(yi, dc, rhs, oc),
                            wait=wwait if dc == 0 else [],
                            sem="PE" if dc == 15 else None)
                    if th == 1:
                        ev_wocons[oc] = last
                    xsl = X32[:, oc * NO + th * 512: oc * NO + (th + 1) * 512]
                    ev_ev1[(oc, th)] = p.op("dve", (lambda yi, xsl: lambda e: e.tensor_tensor(
                        xsl, Py[yi][:], xsl, ALU.add))(yi, xsl),
                        wait=[last, ev_x32], sem="DVE")
                    ev_b = p.op("act", (lambda xsl, th: lambda e: e.copy(SCB[th][:], xsl))(xsl, th),
                                wait=[ev_ev1[(oc, th)], ev_sq.get((oc - 1, th))], sem="ACT")
                    ev_q = p.op("act", (lambda xsl, th: lambda e: e.activation(
                        SQB[th][:], xsl, AF.Square))(xsl, th), sem="ACT")
                    pend_st.append((th, oc, ev_b, ev_q))
                if oc + 2 < 16:
                    dma_wo(oc + 2)
            while pend_st:
                flush_stats()
            ev_ln1 = emit_ln("1", ev_sq[(15, 1)], True, Pst, RowL, RowBL)

        st_ao.close()
        p.barrier()

        Wg = [sb(f"Wg{i}", [128, 2048], BF16) for i in range(2)]
        Wu = [sb(f"Wu{i}", [128, 2048], BF16) for i in range(2)]
        Wd = [sb(f"Wd{i}", [128, G * 2048], BF16) for i in range(2)]
        At = [sb(f"At{i}", [128, G * NO], BF16) for i in range(2)]
        Gs = [sb(f"Gs{i}", [128, 512], BF16) for i in range(2)]
        CsH = sb("CsH", [1, NO], BF16)
        CsL = sb("CsL", [1, NO], BF16)
        AsH = sb("AsH", [1, NO], BF16)
        AsL = sb("AsL", [1, NO], BF16)

        with ExitStack() as ps_b0:
            Pcs = ps_b0.enter_context(nc.psum_tensor("Pcs", [1, NO], F32))
            last = None
            for th in range(2):
                for dc in range(16):
                    last = p.op("pe", (lambda th, dc: lambda e: e.matmul(
                        Pcs[:, th * 512:(th + 1) * 512], OnesB[:],
                        Xb[:, dc * NO + th * 512: dc * NO + (th + 1) * 512],
                        start=(dc == 0), stop=(dc == 15)))(th, dc),
                        wait=[ev_ln1[(dc, th)]],
                        sem="PE" if dc == 15 else None)
            ev_csmm = last
            ev_csh = p.op("act", lambda e: e.copy(CsH[:], Pcs[:]), wait=[ev_csmm], sem="ACT")
            ev_csl = p.op("dve", lambda e: e.tensor_tensor(CsL[:], Pcs[:], CsH[:], ALU.subtract),
                          wait=[ev_csmm, ev_csh], sem="DVE")

        with ExitStack() as ps_b:
            Pas = ps_b.enter_context(nc.psum_tensor("Pas", [1, NO], F32))
            Pg = [ps_b.enter_context(nc.psum_tensor(f"Pg{i}", [128, 512], F32)) for i in range(2)]
            Pu = [ps_b.enter_context(nc.psum_tensor(f"Pu{i}", [128, 512], F32)) for i in range(2)]
            Pd = [ps_b.enter_context(nc.psum_tensor(f"Pd{i}", [128, 512], F32)) for i in range(2)]

            ev_fw = {}
            ev_gate = {}
            ev_up = {}
            ev_silu = {}
            ev_a = {}
            ev_asum = [None] * NH
            ev_dd = [None] * NG
            ev_evac = {}

            def dma_ffn(c):
                if c >= 2:
                    wg_war = ev_gate[(c - 2, 1)]
                    wu_war = ev_up[(c - 2, 1)]
                    assert wg_war is not None and wu_war is not None
                else:
                    wg_war = wu_war = ev_wocons[15]
                gprev = c // G - 2
                if gprev >= 0:
                    wd_war = ev_dd[gprev]
                    assert wd_war is not None
                else:
                    wd_war = ev_wocons[15]
                e1 = p.dma("qg", "FG", (lambda c: lambda e: e.dma_start(Wg[c % 2][:], wg_d[c]))(c),
                           wait=[wg_war])
                e2 = p.dma("qg", "FU", (lambda c: lambda e: e.dma_start(Wu[c % 2][:], wu_d[c]))(c),
                           wait=[wu_war])
                e3 = p.dma("qg", "FD", (lambda c: lambda e: e.dma_start(
                    Wd[(c // G) % 2][:, (c % G) * 2048:(c % G + 1) * 2048], wd_d[c]))(c),
                    wait=[wd_war])
                ev_fw[c] = (e1, e2, e3)

            dma_ffn(0)
            dma_ffn(1)

            def emit_asum(c):
                gi = (c // G) % 2
                sl = c % G
                for th in range(2):
                    ev_asum[c] = p.op("pe", (lambda gi, sl, th, c: lambda e: e.matmul(
                        Pas[:, th * 512:(th + 1) * 512], OnesB[:],
                        At[gi][:, sl * NO + th * 512: sl * NO + (th + 1) * 512],
                        start=(c == 0), stop=(c == NH - 1)))(gi, sl, th, c),
                        wait=[ev_a[(c, th)]], sem="PE")

            for c in range(NH):
                gg = c // G
                gi = gg % 2
                sl = c % G
                if c >= 2:
                    emit_asum(c - 2)
                for th in range(2):
                    bi = th
                    gw = [ev_fw[c][0], ev_csl, ev_csh, ev_silu.get((c - 1, th))]
                    for dc in range(16):
                        p.op("pe", (lambda bi, dc, c, th: lambda e: e.matmul(
                            Pg[bi][:], Wg[c % 2][:, dc * 128:(dc + 1) * 128],
                            Xb[:, dc * NO + th * 512: dc * NO + (th + 1) * 512],
                            start=(dc == 0), stop=False))(bi, dc, c, th),
                            wait=gw if dc == 0 else [])
                    p.op("pe", (lambda bi, th: lambda e: e.matmul(
                        Pg[bi][:], NegB[:], CsH[:, th * 512:(th + 1) * 512],
                        start=False, stop=False))(bi, th))
                    ev_gate[(c, th)] = p.op("pe", (lambda bi, th: lambda e: e.matmul(
                        Pg[bi][:], NegB[:], CsL[:, th * 512:(th + 1) * 512],
                        start=False, stop=True))(bi, th), sem="PE")
                    uw = [ev_fw[c][1], ev_a.get((c - 1, th))]
                    for dc in range(16):
                        p.op("pe", (lambda bi, dc, c, th: lambda e: e.matmul(
                            Pu[bi][:], Wu[c % 2][:, dc * 128:(dc + 1) * 128],
                            Xb[:, dc * NO + th * 512: dc * NO + (th + 1) * 512],
                            start=(dc == 0), stop=False))(bi, dc, c, th),
                            wait=uw if dc == 0 else [])
                    p.op("pe", (lambda bi, th: lambda e: e.matmul(
                        Pu[bi][:], NegB[:], CsH[:, th * 512:(th + 1) * 512],
                        start=False, stop=False))(bi, th))
                    ev_up[(c, th)] = p.op("pe", (lambda bi, th: lambda e: e.matmul(
                        Pu[bi][:], NegB[:], CsL[:, th * 512:(th + 1) * 512],
                        start=False, stop=True))(bi, th), sem="PE")
                    ev_silu[(c, th)] = p.op("act", (lambda bi, c, th: lambda e: e.activation(
                        Gs[th][:], Pg[bi][:], AF.Silu, scale=Sg[:, c:c + 1]))(bi, c, th),
                        wait=[ev_gate[(c, th)], ev_sg, ev_a.get((c - 1, th))], sem="ACT")
                    at_war = ev_dd[gg - 2] if gg >= 2 else None
                    at_war2 = ev_asum[c - 2 * G] if c >= 2 * G else None
                    ev_a[(c, th)] = p.op("dve", (lambda bi, gi, sl, c, th: lambda e: e.scalar_tensor_tensor(
                        At[gi][:, sl * NO + th * 512: sl * NO + (th + 1) * 512],
                        Pu[bi][:], Su[:, c:c + 1], Gs[th][:],
                        ALU.mult, ALU.mult))(bi, gi, sl, c, th),
                        wait=[ev_up[(c, th)], ev_silu[(c, th)], at_war, at_war2, ev_su], sem="DVE")
                if sl == G - 1:
                    lastg = (gg == NG - 1)
                    if lastg:
                        emit_asum(c - 1)
                        emit_asum(c)
                        ev_ash = p.op("act", lambda e: e.copy(AsH[:], Pas[:]),
                                      wait=[ev_asum[c]], sem="ACT")
                        ev_asl = p.op("dve", lambda e: e.tensor_tensor(AsL[:], Pas[:], AsH[:], ALU.subtract),
                                      wait=[ev_asum[c], ev_ash], sem="DVE")
                    for j in range(ND):
                        for th in range(2):
                            pj = th
                            dwait = [ev_a[(c, th)], ev_fw[c][2], ev_fw[c - 1][2],
                                     ev_evac.get((gg, j - 1, th))]
                            if j == 0 and gg >= 1:
                                dwait.append(ev_evac.get((gg - 1, ND - 1, th)))
                            rm = None
                            for s2 in range(G):
                                rm = p.op("pe", (lambda pj, gi, s2, j, th, lastg: lambda e: e.matmul(
                                    Pd[pj][:], Wd[gi][:, (s2 * ND + j) * 128:(s2 * ND + j + 1) * 128],
                                    At[gi][:, s2 * NO + th * 512: s2 * NO + (th + 1) * 512],
                                    start=(s2 == 0), stop=(s2 == G - 1) and not lastg))(pj, gi, s2, j, th, lastg),
                                    wait=dwait if s2 == 0 else [],
                                    sem="PE" if (s2 == G - 1 and not lastg) else None)
                            if lastg:
                                if j == 0 and th == 0:
                                    p.wait_only("pe", [ev_ash, ev_asl])
                                p.op("pe", (lambda pj, th: lambda e: e.matmul(
                                    Pd[pj][:], NegB[:], AsH[:, th * 512:(th + 1) * 512],
                                    start=False, stop=False))(pj, th))
                                rm = p.op("pe", (lambda pj, th: lambda e: e.matmul(
                                    Pd[pj][:], NegB[:], AsL[:, th * 512:(th + 1) * 512],
                                    start=False, stop=True))(pj, th), sem="PE")
                            xsl = X32[:, j * NO + th * 512: j * NO + (th + 1) * 512]
                            ev_evac[(gg, j, th)] = p.op("dve", (lambda pj, j, xsl: lambda e: e.scalar_tensor_tensor(
                                xsl, Pd[pj][:], Sd[:, j:j + 1], xsl, ALU.mult, ALU.add))(pj, j, xsl),
                                wait=[rm, ev_sd], sem="DVE")
                    ev_dd[gg] = ("PE", p.cnt["PE"])
                if c + 2 < NH:
                    dma_ffn(c + 2)

        p.barrier()

        with ExitStack() as ps_c:
            PstC = ps_c.enter_context(nc.psum_tensor("PstC", [1, 4 * 512], F32))
            ev_sq = {}
            for oc in range(16):
                for th in range(2):
                    xsl = X32[:, oc * NO + th * 512: oc * NO + (th + 1) * 512]
                    ev_b = p.op("act", (lambda xsl, th: lambda e: e.copy(SCB[th][:], xsl))(xsl, th),
                                wait=[ev_evac[(NG - 1, oc, th)], ev_sq.get((oc - 1, th))], sem="ACT")
                    ev_q = p.op("act", (lambda xsl, th: lambda e: e.activation(
                        SQB[th][:], xsl, AF.Square))(xsl, th), sem="ACT")
                    p.op("pe", (lambda th, oc: lambda e: e.matmul(
                        PstC[:, th * 512:(th + 1) * 512], OnesB[:], SCB[th][:],
                        start=(oc == 0), stop=(oc == 15)))(th, oc),
                        wait=[ev_b])
                    ev_sq[(oc, th)] = p.op("pe", (lambda th, oc: lambda e: e.matmul(
                        PstC[:, (2 + th) * 512:(3 + th) * 512], OnesB[:], SQB[th][:],
                        start=(oc == 0), stop=(oc == 15)))(th, oc),
                        wait=[ev_q], sem="PE")
            ev_ln2 = emit_ln("2", ev_sq[(15, 1)], False, PstC, RowL, RowBL)

        ev_os = []
        for oc in range(16):
            ev_os.append(p.dma("qs", "OS", (lambda oc: lambda e: e.dma_start(
                out_d[:, oc * NO:(oc + 1) * NO], X32[:, oc * NO:(oc + 1) * NO]))(oc),
                wait=[ev_ln2[(oc, 0)], ev_ln2[(oc, 1)]]))
        p.wait_only("qs", ev_os[-2:])

        block = st.enter_context(nc.Block())

        @block.tensor
        def _(t):
            for f in p.ops["pe"]:
                f(t)

        @block.scalar
        def _(s_):
            for f in p.ops["act"]:
                f(s_)

        @block.vector
        def _(v):
            for f in p.ops["dve"]:
                f(v)

        @block.gpsimd
        def _(g_):
            for f in p.ops["qg"]:
                f(g_)

        @block.sync
        def _(sy):
            for f in p.ops["qs"]:
                f(sy)
    return nc



def _feat_major(mat):
    T = mat.shape[0]
    return np.ascontiguousarray(mat.T.reshape(ND, 128, T).transpose(1, 0, 2).reshape(128, ND * T))


def _wblocks(w, nch_out):
    Din, F = w.shape
    ndc = Din // 128
    return np.ascontiguousarray(
        w.reshape(ndc, 128, nch_out, F // nch_out).transpose(2, 1, 0, 3)
        .reshape(nch_out, 128, ndc * (F // nch_out)))


def _rope_tables(positions):
    half = 32
    inv = (1.0 / THETA ** (np.arange(half, dtype=np.float32) / half)).astype(np.float32)
    pos = np.asarray(positions, dtype=np.float32)
    fr = np.outer(inv, pos).astype(np.float32)
    c = np.cos(fr).astype(np.float32)
    s = np.sin(fr).astype(np.float32)
    C = np.tile(np.concatenate([c, c], 0), (2, 1))
    Sm = np.tile(np.concatenate([-s, s], 0), (2, 1))
    return np.ascontiguousarray(C), np.ascontiguousarray(Sm)


_PROG_CACHE = {}


def kernel(L0_attn_x, L0_attn_Wq, L0_attn_Wk, L0_attn_Wv, L0_attn_Wo,
           L0_ffn_wq_gate, L0_ffn_s_gate, L0_ffn_wq_up, L0_ffn_s_up,
           L0_ffn_wq_down, L0_ffn_s_down):
    x = np.asarray(L0_attn_x, np.float32)
    Wq = np.asarray(L0_attn_Wq, np.float32)
    Wk = np.asarray(L0_attn_Wk, np.float32)
    Wv = np.asarray(L0_attn_Wv, np.float32)
    Wo = np.asarray(L0_attn_Wo, np.float32)
    wqg = np.ascontiguousarray(np.asarray(L0_ffn_wq_gate, np.int32))
    wqu = np.ascontiguousarray(np.asarray(L0_ffn_wq_up, np.int32))
    wqd = np.ascontiguousarray(np.asarray(L0_ffn_wq_down, np.int32))
    s_g = np.asarray(L0_ffn_s_gate, np.float32)
    s_u = np.asarray(L0_ffn_s_up, np.float32)
    s_d = np.asarray(L0_ffn_s_down, np.float32)

    perm_q = np.empty(D, np.int64)
    perm_o = np.empty(D, np.int64)
    for s_ in range(NHQ):
        h = SLOT_HEADS[s_]
        for j in range(64):
            perm_q[s_ * 64 + j] = h * 64 + ROPE_PERM[j]
            perm_o[s_ * 64 + j] = h * 64 + j
    perm_k = np.empty(NKV * 64, np.int64)
    for g in range(NKV):
        for j in range(64):
            perm_k[g * 64 + j] = g * 64 + ROPE_PERM[j]

    psw = np.zeros((128, 128), np.float32)
    for i in range(128):
        psw[i ^ 32, i] = 1.0

    consts = {
        "wq": _wblocks(Wq[:, perm_q], 16),
        "wk": _wblocks(Wk[:, perm_k], 4),
        "wv": np.ascontiguousarray(Wv.reshape(16, 128, 512)),
        "wo": _wblocks(Wo[perm_o, :], 16),
        "wg": np.ascontiguousarray(wqg.reshape(NH, 128, ND, 128).transpose(0, 3, 2, 1).reshape(NH, 128, ND * 128)),
        "wu": np.ascontiguousarray(wqu.reshape(NH, 128, ND, 128).transpose(0, 3, 2, 1).reshape(NH, 128, ND * 128)),
        "wd": np.ascontiguousarray(wqd.reshape(ND, 128, NH, 128).transpose(2, 3, 0, 1).reshape(NH, 128, ND * 128)),
        "sg": np.ascontiguousarray(s_g.reshape(NH, 128).T),
        "su": np.ascontiguousarray(s_u.reshape(NH, 128).T),
        "sd": np.ascontiguousarray(s_d.reshape(ND, 128).T),
        "onesb": np.ones((128, 1), np.float32),
        "negb": np.full((1, 128), -128.0, np.float32),
        "onesr": np.ones((1, 128), np.float32),
        "ones64": np.ones((1, 64), np.float32),
        "psw": psw,
    }

    in_maps = []
    own_all = []
    for core in range(8):
        b = core // 2
        half = core % 2
        blocks = _own_blocks(half)
        own = np.concatenate([np.arange(s0, s1) for s0, s1 in blocks])
        rest = np.setdiff1d(np.arange(S), own)
        key_order = np.concatenate([own, rest])
        own_all.append((b, own))
        x_dev = _feat_major(x[b][key_order])
        ck, sk = _rope_tables(key_order)
        m = np.zeros((128, 16 * 512), np.float32)
        tiles = [(0, kc) for kc in MASKABLE0] + [(1, kc) for kc in MASKABLE1]
        for t_i, (b_, kc) in enumerate(tiles):
            qpos = own[b_ * 512:(b_ + 1) * 512]
            kpos = key_order[kc * 128:(kc + 1) * 128]
            m[:, t_i * 512:(t_i + 1) * 512] = (kpos[:, None] <= qpos[None, :])
        in_maps.append({
            "x": x_dev, "ck": ck, "sk": sk,
            "msk": m, **consts,
        })

    if "nc" not in _PROG_CACHE:
        _PROG_CACHE["nc"] = build_program()
    nc = _PROG_CACHE["nc"]
    res = run_bass_kernel_spmd(nc, in_maps, core_ids=list(range(8)))

    out = np.empty((B, S, D), np.float32)
    for core in range(8):
        b, own = own_all[core]
        o = np.asarray(res.results[core]["out"], np.float32)
        o = o.reshape(128, ND, NO).transpose(1, 0, 2).reshape(D, NO)
        out[b, own, :] = o.T
    return out
